# version 02
# baseline (speedup 1.0000x reference)
import numpy as np
import ml_dtypes
from contextlib import ExitStack
from collections import defaultdict

import concourse.bass as bass
import concourse.mybir as mybir
from concourse.bass import IndirectOffsetOnAxis
from concourse.bass_utils import run_bass_kernel_spmd

F32 = mybir.dt.float32
BF16 = mybir.dt.bfloat16
U32 = mybir.dt.uint32
AF = mybir.ActivationFunctionType
ALU = mybir.AluOpType
AX = mybir.AxisListType
BF = ml_dtypes.bfloat16

NCORE = 8
C, T, E = 4, 4096, 1024
H, D = 16, 64
NE, KTOP, CAP, HID = 8, 2, 4096, 4096
TPC = T // NCORE
NTOK = C * TPC
NTILE = NTOK // 128
RG = 256
NRG = NTOK // RG
NSG = CAP // RG
EPS = 1e-5

_o = 0
def _blk(w):
    global _o
    s = _o; _o += w; return s
CO_IDENT = _blk(128)
CO_L128 = _blk(128)
CO_IOTA8 = _blk(128)
CO_REV8 = _blk(128)
CO_E16 = _blk(128)
CO_UCOLS = _blk(256)
CO_OOB = _blk(16)
CO_VALB = _blk(16)
CO_C4 = _blk(4)
CO_E4 = _blk(16)
CO_LBLK = _blk(16)
CO_WJ = _blk(4)
CO_ONES = _blk(128)
CO_EPS = _blk(1)
CO_I32 = _blk(1)
CPW_SB = _o
CO_ZERO = _blk(128)
CPW = _o


def make_constpack(j):
    cp = np.zeros((128, CPW), np.float32)
    cp[:, CO_IDENT:CO_IDENT + 128] = np.eye(128, dtype=np.float32)
    L = np.zeros((128, 128), np.float32)
    for i in range(128):
        L[:i, i] = 1.0
    cp[:, CO_L128:CO_L128 + 128] = L
    col = np.arange(128)
    cp[:, CO_IOTA8:CO_IOTA8 + 128] = np.tile((col % 8).astype(np.float32), (128, 1))
    cp[:, CO_REV8:CO_REV8 + 128] = np.tile((8.0 - (col % 8)).astype(np.float32), (128, 1))
    e16 = np.zeros((128, 128), np.float32)
    for c in range(8):
        e16[0:64, 16 * c + 2 * c] = 1.0
        e16[64:128, 16 * c + 2 * c + 1] = 1.0
    cp[:, CO_E16:CO_E16 + 128] = e16
    uc = np.zeros((128, 256), np.float32)
    for t in range(16):
        uc[:, 16 * t + t] = 1.0
    cp[:, CO_UCOLS:CO_UCOLS + 256] = uc
    cp[:, CO_OOB:CO_OOB + 16] = (40000 + np.arange(128)[:, None] * 16 + np.arange(16)[None, :]).astype(np.float32)
    cp[:, CO_VALB:CO_VALB + 16] = (j * NTOK + np.arange(16)[None, :] * 128 + np.arange(128)[:, None] + 1).astype(np.float32)
    c4 = np.zeros((128, 4), np.float32)
    for t in range(16):
        c4[t, t // 4] = 1.0
    cp[:, CO_C4:CO_C4 + 4] = c4
    e4 = np.zeros((128, 16), np.float32)
    for t in range(16):
        e4[t // 4, t] = 1.0
    cp[:, CO_E4:CO_E4 + 16] = e4
    lb = np.zeros((128, 16), np.float32)
    for t in range(16):
        for tp in range(16):
            if tp < t and tp // 4 == t // 4:
                lb[tp, t] = 1.0
    cp[:, CO_LBLK:CO_LBLK + 16] = lb
    wj = np.zeros((128, 4), np.float32)
    for jp in range(NCORE):
        for cprev in range(4):
            for c_ in range(4):
                if cprev < c_ or (cprev == c_ and jp < j):
                    wj[jp * 4 + cprev, c_] = 1.0
    cp[:, CO_WJ:CO_WJ + 4] = wj
    cp[:, CO_ONES:CO_ONES + 128] = 1.0
    cp[:, CO_EPS] = EPS
    cp[:, CO_I32] = 1.0 / 32.0
    return cp


def build(dbg=False):
    nc = bass.Bass("TRN2", target_bir_lowering=False)

    x_my = nc.dram_tensor("x_my", [NTOK, E], F32, kind="ExternalInput").ap()
    wq_d = nc.dram_tensor("wq", [D, D], F32, kind="ExternalInput").ap()
    wk_d = nc.dram_tensor("wk", [D, D], F32, kind="ExternalInput").ap()
    wv_d = nc.dram_tensor("wv", [D, D], F32, kind="ExternalInput").ap()
    rw1_d = nc.dram_tensor("rw1r", [128, 8 * HID], F32, kind="ExternalInput").ap()
    rw2_d = nc.dram_tensor("rw2r", [128, 32 * NE], F32, kind="ExternalInput").ap()
    ew1_d = nc.dram_tensor("ew1r", [128, 8 * HID], F32, kind="ExternalInput").ap()
    ew2_d = nc.dram_tensor("ew2r", [128, 32 * E], F32, kind="ExternalInput").ap()
    rb1_d = nc.dram_tensor("rb1r", [128, 32], F32, kind="ExternalInput").ap()
    eb1_d = nc.dram_tensor("eb1r", [128, 32], F32, kind="ExternalInput").ap()
    eb2_d = nc.dram_tensor("eb2r", [128, 8], F32, kind="ExternalInput").ap()
    rb2_d = nc.dram_tensor("rb2c", [8, 1], F32, kind="ExternalInput").ap()
    cp_d = nc.dram_tensor("cpk", [128, CPW], F32, kind="ExternalInput").ap()

    out_d = nc.dram_tensor("out", [NTOK, E], F32, kind="ExternalOutput").ap()

    ln2my = nc.dram_tensor("ln2my", [NTOK, E], BF16).ap()
    ln2all = nc.dram_tensor("ln2all", [NCORE * NTOK, E], BF16).ap()
    x1my = nc.dram_tensor("x1my", [NTOK, E], F32).ap()
    xsb = nc.dram_tensor("xsb", [4, E], F32).ap()
    xsr = nc.dram_tensor("xsr", [4, E], F32).ap()
    vscr = nc.dram_tensor("vscr", [4, E], F32).ap()
    mscr = nc.dram_tensor("mscr", [64, 64], F32).ap()
    cntb = nc.dram_tensor("cntb", [4, NE], F32).ap()
    cntg = nc.dram_tensor("cntg", [4 * NCORE, NE], F32).ap()
    rscr = nc.dram_tensor("rscr", [16, NE], F32).ap()
    tbl_d = nc.dram_tensor("tbl", [NE * CAP], F32).ap()
    tblrs = nc.dram_tensor("tblrs", [32, 128], F32).ap()
    dispD = nc.dram_tensor("dispD", [CAP, E], BF16).ap()
    eoD = nc.dram_tensor("eoD", [E, CAP], BF16).ap()
    eomy = nc.dram_tensor("eomy", [CAP, E], BF16).ap()
    eoall = nc.dram_tensor("eoall", [NCORE * CAP, E], BF16).ap()
    if dbg:
        d_sc = nc.dram_tensor("d_sc", [16, 2048], F32, kind="ExternalOutput").ap()
        d_sm = nc.dram_tensor("d_sm", [128, 300], F32, kind="ExternalOutput").ap()
        d_ptm = nc.dram_tensor("d_ptm", [128, 256], F32, kind="ExternalOutput").ap()
        d_vscr = nc.dram_tensor("d_vscr", [4, E], F32, kind="ExternalOutput").ap()
        d_xsr = nc.dram_tensor("d_xsr", [4, E], F32, kind="ExternalOutput").ap()
        d_x1 = nc.dram_tensor("d_x1", [NTOK, E], F32, kind="ExternalOutput").ap()
        d_ln2 = nc.dram_tensor("d_ln2", [NTOK, E], BF16, kind="ExternalOutput").ap()
        d_lg = nc.dram_tensor("d_lg", [128, 128], F32, kind="ExternalOutput").ap()
        d_pos = nc.dram_tensor("d_pos", [128, 64], F32, kind="ExternalOutput").ap()
        d_tbl = nc.dram_tensor("d_tbl", [32, 128], F32, kind="ExternalOutput").ap()
        d_eo = nc.dram_tensor("d_eo", [CAP, E], BF16, kind="ExternalOutput").ap()

    ES = ExitStack()
    def sb(name, shape, dt):
        return ES.enter_context(nc.sbuf_tensor(name, shape, dt))

    cpk = sb("cpk_sb", [128, CPW_SB], F32)
    IDENT = cpk[:, CO_IDENT:CO_IDENT + 128]
    L128 = cpk[:, CO_L128:CO_L128 + 128]
    IOTA8 = cpk[:, CO_IOTA8:CO_IOTA8 + 128]
    REV8 = cpk[:, CO_REV8:CO_REV8 + 128]
    E16 = cpk[:, CO_E16:CO_E16 + 128]
    UCOLS = cpk[:, CO_UCOLS:CO_UCOLS + 256]
    OOBC = cpk[:, CO_OOB:CO_OOB + 16]
    VALB = cpk[:, CO_VALB:CO_VALB + 16]
    C4 = cpk[0:16, CO_C4:CO_C4 + 4]
    E4 = cpk[0:4, CO_E4:CO_E4 + 16]
    LBLK = cpk[0:16, CO_LBLK:CO_LBLK + 16]
    WJ = cpk[0:32, CO_WJ:CO_WJ + 4]
    ONES1 = cpk[0:1, CO_ONES:CO_ONES + 128]
    EPSC = cpk[:, CO_EPS:CO_EPS + 1]
    I32C = cpk[:, CO_I32:CO_I32 + 1]

    wq = sb("wq_sb", [128, 64], F32)
    wk = sb("wk_sb", [128, 64], F32)
    wv = sb("wv_sb", [64, 64], F32)
    wvT = sb("wvT_sb", [64, 64], F32)
    wvblk = sb("wvblk_sb", [128, 128], F32)
    m_sb = sb("m_sb", [64, 64], F32)
    mblk = sb("mblk_sb", [128, 128], F32)
    rb1 = sb("rb1_sb", [128, 32], F32)
    eb1 = sb("eb1_sb", [128, 32], F32)
    eb2 = sb("eb2_sb", [128, 8], F32)
    rb2 = sb("rb2_sb", [8, 1], F32)
    rw2 = sb("rw2_sb", [128, 32 * NE], BF16)

    AXA = sb("axa", [128, 16 * E], F32)
    xt_t = lambda t: AXA[:, E * t:E * (t + 1)]
    ew1 = AXA.bitcast(BF16)
    WBA = sb("wba", [128, 8 * HID], BF16)
    HB = sb("hb", [128, 32 * RG], BF16)
    HBf = HB.bitcast(F32)
    FMA = sb("fma", [128, 2 * 8 * RG], BF16)
    LFA = sb("lfa", [128, 8 * 512], F32)
    LFB = LFA.bitcast(BF16)
    PB = sb("pb", [128, 2 * E], F32)
    scores = PB[0:16, 0:2048]
    probsb = HBf[0:16, 0:2048]
    x1_t = lambda s: PB[:, E * s:E * (s + 1)]
    ln1c = sb("ln1c", [128, 2 * 128], F32)
    tmp1 = sb("tmp1", [128, E], F32)
    sq = sb("sq", [128, E], F32)
    ln2t = sb("ln2t", [128, 2 * E], BF16)
    stats = sb("stats", [128, 32], F32)
    probs_tm = sb("probs_tm", [128, 16 * 16], F32)
    vsr2 = sb("vsr2", [1, E], F32)
    xsum = sb("xsum", [128, 32], F32)
    xsT = sb("xsT", [32, 128], F32)
    xc = sb("xc", [128, 8], F32)
    veh = sb("veh", [128, 8], F32)
    te_b = sb("te_b", [8, 128], F32)
    den = HBf[0:16, 2048:3072]
    ltT = sb("ltT", [128, 128], F32)
    a1 = sb("a1", [128, 128], F32)
    a2 = sb("a2", [128, 128], F32)
    bsum = sb("bsum", [128, 128], F32)
    pfull = sb("pfull", [128, 128], F32)
    m1s = sb("m1s", [128, 16], F32)
    m2s = sb("m2s", [128, 16], F32)
    i1s = sb("i1s", [128, 16], F32)
    i2s = sb("i2s", [128, 16], F32)
    p1s = sb("p1s", [128, 16], F32)
    p2s = sb("p2s", [128, 16], F32)
    w1s = sb("w1s", [128, 16], F32)
    w2s = sb("w2s", [128, 16], F32)
    sc1 = sb("sc1", [128, 16], F32)
    sc2 = sb("sc2", [128, 16], F32)
    of1 = sb("of1", [128, 16], U32)
    of2 = sb("of2", [128, 16], U32)
    du1 = sb("du1", [128, 16], U32)
    du2 = sb("du2", [128, 16], U32)
    colsum = sb("colsum", [16, 8], F32)
    cnt_sb = sb("cnt_sb", [4, 8], F32)
    cag = sb("cag", [32, 8], F32)
    base = sb("base", [4, 8], F32)
    rall = sb("rall", [16, 8], F32)
    rrow = sb("rrow", [1, 128], F32)
    lgsb = sb("lgsb", [8, RG], F32)
    prod = FMA.bitcast(F32)[:, 0:1024]
    tblT = tmp1[:, 640:672]
    tmsk = tmp1[:, 672:704]
    idxg = tmp1[:, 704:736]
    idxu = sb("idxu", [128, 32], U32)
    tblr = ln1c[0:32, 0:128]

    ps = [ES.enter_context(nc.psum_tensor(f"ps{i}", [128, 512], F32)) for i in range(8)]

    sem_names = ["dS", "dG", "tPE", "tAC", "tVE", "cXS", "cCN", "cL2", "cTB", "cEO"]
    SEM = {n: ES.enter_context(nc.semaphore(n)) for n in sem_names}

    EV = []
    CNT = defaultdict(int)
    REGS = {}

    def op(eng, emit, waits=(), inc=None, amt=1):
        tok = None
        if inc is not None:
            CNT[inc] += amt
            tok = (inc, CNT[inc])
        EV.append((eng, tuple(w for w in waits if w is not None), emit, (inc, amt) if inc else None))
        return tok

    def dma(eng_code, out, in_, waits=(), sem="dS", **kw):
        return op(eng_code, lambda e: e.dma_start(out=out, in_=in_, **kw), waits, inc=sem, amt=16)

    tk_cp = dma("gp", cpk[:, :], cp_d[:, 0:CPW_SB], sem="dG")
    tk_zq = dma("sy", wq[:, :], cp_d[:, CO_ZERO:CO_ZERO + 64])
    tk_zk = dma("sy", wk[:, :], cp_d[:, CO_ZERO:CO_ZERO + 64])
    tk_wq = dma("sy", wq[0:64, :], wq_d, waits=[tk_zq])
    tk_wk = dma("sy", wk[0:64, :], wk_d, waits=[tk_zk])
    tk_wv = dma("gp", wv[:, :], wv_d, sem="dG")
    tk_rb1 = dma("gp", rb1[:, :], rb1_d, sem="dG")
    tk_eb1 = dma("gp", eb1[:, :], eb1_d, sem="dG")
    tk_eb2 = dma("gp", eb2[:, :], eb2_d, sem="dG")
    tk_rb2 = dma("gp", rb2[:, :], rb2_d, sem="dG")
    tk_rw2 = dma("gp", rw2[:, :], rw2_d, sem="dG")
    for k in range(8):
        dma("gp", WBA[:, HID * k:HID * (k + 1)].rearrange("p (s q) -> p s q", q=2048),
            rw1_d[:, HID * k:HID * (k + 1)].rearrange("p (s q) -> p s q", q=2048), sem="dG")
    tk_rw1 = ("dG", CNT["dG"])

    tk_m = op("te", lambda e: e.matmul(ps[6][0:64, 0:64], wq[:, :], wk[:, :], start=True, stop=True, skip_group_check=True),
              waits=[tk_wq, tk_wk], inc="tPE")
    tk_me = op("ac", lambda e: e.copy(m_sb[:, :], ps[6][0:64, 0:64]), waits=[tk_m], inc="tAC")
    tk_mz = dma("sy", mblk[:, :], cp_d[:, CO_ZERO:CO_ZERO + 128])
    tk_mb1 = op("ac", lambda e: e.copy(mblk[0:64, 0:64], m_sb[:, :]), waits=[tk_me, tk_mz], inc="tAC")
    tk_mst = dma("sy", mscr, m_sb[:, :], waits=[tk_me])
    tk_mb2 = dma("sy", mblk[64:128, 64:128], mscr, waits=[tk_mst, tk_mz])
    tk_wvt = op("te", lambda e: e.transpose(ps[7][0:64, 0:64], wv[:, :], IDENT[0:64, 0:64]),
                waits=[tk_wv, tk_cp, tk_m], inc="tPE")
    tk_wvte1 = op("ac", lambda e: e.copy(wvT[:, :], ps[7][0:64, 0:64]), waits=[tk_wvt], inc="tAC")
    tk_zv = dma("sy", wvblk[:, :], cp_d[:, CO_ZERO:CO_ZERO + 128])
    tk_wb1 = op("ac", lambda e: e.copy(wvblk[0:64, 0:64], wvT[:, :]), waits=[tk_wvte1, tk_zv], inc="tAC")
    tk_wvst = dma("sy", mscr, wvT[:, :], waits=[tk_wvte1, tk_mb2])
    tk_wvld = dma("sy", wvblk[64:128, 64:128], mscr, waits=[tk_wvst, tk_zv])

    tk_xld = [None] * NTILE
    scores_evac = [None] * 4
    fm_last = {}
    tr_war = {}
    ln1war = {}
    gwar = {}
    prodwar = {}
    sqwar = None
    for g in range(4):
        e16_first_wait = scores_evac[g - 1] if g > 0 else None
        for tl in range(4):
            tau = 4 * g + tl
            tk_xld[tau] = dma("sy", xt_t(tau), x_my[128 * tau:128 * (tau + 1), :])
            st = stats[:, 8 * (tau % 4):8 * (tau % 4) + 8]
            t_sq = op("ac", lambda e, tau=tau: e.activation(sq[:, :], xt_t(tau), AF.Square),
                      waits=[tk_xld[tau], sqwar], inc="tAC")
            t_s1 = op("ve", lambda e, st=st, tau=tau: e.reduce_sum(st[:, 0:1], xt_t(tau), axis=AX.X),
                      waits=[tk_xld[tau]], inc="tVE")
            t_s2 = op("ve", lambda e, st=st: e.reduce_sum(st[:, 1:2], sq[:, :], axis=AX.X),
                      waits=[t_sq], inc="tVE")
            sqwar = t_s2
            t_mn = op("ve", lambda e, st=st: e.tensor_scalar_mul(st[:, 2:3], st[:, 0:1], 1.0 / E),
                      waits=[t_s1], inc="tVE")
            t_ms = op("ve", lambda e, st=st: e.tensor_mul(st[:, 4:5], st[:, 2:3], st[:, 2:3]),
                      waits=[t_mn], inc="tVE")
            t_vr = op("ve", lambda e, st=st: e.scalar_tensor_tensor(
                st[:, 5:6], st[:, 1:2], 1.0 / E, st[:, 4:5], ALU.mult, ALU.subtract),
                waits=[t_ms], inc="tVE")
            t_sd = op("ac", lambda e, st=st: e.activation(st[:, 6:7], st[:, 5:6], AF.Sqrt, bias=EPSC[0:128, :]),
                      waits=[t_vr], inc="tAC")
            t_rs = op("ve", lambda e, st=st: e.reciprocal(st[:, 3:4], st[:, 6:7]),
                      waits=[t_sd], inc="tVE")
            for c in range(8):
                slot = c % 2
                t_ap = op("ve", lambda e, tau=tau, c=c, slot=slot, st=st: e.tensor_scalar(
                    ln1c[:, 128 * slot:128 * (slot + 1)], xt_t(tau)[:, 128 * c:128 * (c + 1)],
                    st[:, 2:3], st[:, 3:4], ALU.subtract, ALU.mult),
                    waits=[t_rs, ln1war.get(slot)], inc="tVE")
                par = c % 2
                t_tr = op("te", lambda e, par=par, slot=slot: e.transpose(
                    ps[par][:, 0:128], ln1c[:, 128 * slot:128 * (slot + 1)], IDENT),
                    waits=[t_ap, tr_war.get(par)], inc="tPE")
                ln1war[slot] = t_tr
                t_ev = op("ac", lambda e, c=c, tl=tl, par=par: e.copy(
                    LFA[:, 512 * c + 128 * tl:512 * c + 128 * (tl + 1)], ps[par][:, 0:128]),
                    waits=[t_tr], inc="tAC")
                tr_war[par] = t_ev
                if tl == 3:
                    fm_last[(g, c)] = t_ev
        for c in range(8):
            par = 2 + (c % 2)
            t_g = op("te", lambda e, c=c, par=par: e.matmul(
                ps[par][:, 0:512], mblk[:, :], LFA[:, 512 * c:512 * (c + 1)], start=True, stop=True, skip_group_check=True),
                waits=[fm_last[(g, c)], tk_mb1, tk_mb2, gwar.get(par)], inc="tPE")
            t_p = op("ve", lambda e, c=c, par=par: e.tensor_mul(
                prod[:, 512 * (c % 2):512 * (c % 2 + 1)], LFA[:, 512 * c:512 * (c + 1)], ps[par][:, 0:512]),
                waits=[t_g, prodwar.get(c % 2)], inc="tVE")
            gwar[par] = t_p
            t_e16 = op("te", lambda e, c=c: e.matmul(
                ps[4][0:16, 0:512], E16[:, 16 * c:16 * (c + 1)], prod[:, 512 * (c % 2):512 * (c % 2 + 1)],
                start=(c == 0), stop=(c == 7), skip_group_check=True),
                waits=[t_p] + ([e16_first_wait] if c == 0 else []), inc="tPE")
            prodwar[c % 2] = t_e16
            op("ve", lambda e, g=g, c=c: e.reduce_sum(
                xsum[:, 8 * g + c:8 * g + c + 1], LFA[:, 512 * c:512 * (c + 1)], axis=AX.X),
                waits=[fm_last[(g, c)]], inc="tVE")
        scores_evac[g] = op("ac", lambda e, g=g: e.activation(
            scores[:, 512 * g:512 * (g + 1)], ps[4][0:16, 0:512], AF.Copy, scale=I32C[0:16, :]),
            waits=[prodwar[1]], inc="tAC")
    tk_xsum_all = ("tVE", CNT["tVE"])

    t_xt = op("te", lambda e: e.transpose(ps[6][0:32, 0:128], xsum[:, :], IDENT),
              waits=[tk_xsum_all, tk_me], inc="tPE")
    t_xte = op("ac", lambda e: e.copy(xsT[:, :], ps[6][0:32, 0:128]), waits=[t_xt], inc="tAC")
    t_xst = dma("sy", xsb.rearrange("a (k p) -> (a k) p", p=128), xsT[:, :], waits=[t_xte])
    tk_ccxs = op("gp", lambda e: e.collective_compute(
        "AllReduce", ALU.add, replica_groups=[list(range(NCORE))], ins=[xsb], outs=[xsr]),
        waits=[t_xst], inc="cXS")
    vsr_ld = [None] * 4
    ps5war = None
    teb_war = (None, None)
    for cch in range(4):
        t_l = dma("sy", tblr[0:8, :], xsr[cch, :].rearrange("(k p) -> k p", p=128), waits=[tk_ccxs, ps5war])
        t_t = op("te", lambda e: e.transpose(ps[5][:, 0:8], tblr[0:8, :], IDENT[0:8, 0:8]),
                 waits=[t_l, ps5war], inc="tPE")
        t_e = op("ac", lambda e: e.copy(xc[:, :], ps[5][:, 0:8]), waits=[t_t], inc="tAC")
        t_v = op("te", lambda e: e.matmul(ps[5][:, 8:16], wvblk[:, :], xc[:, :],
                                          start=True, stop=True, skip_group_check=True),
                 waits=[t_e, tk_wb1, tk_wvld], inc="tPE")
        t_ve = op("ac", lambda e: e.copy(veh[:, :], ps[5][:, 8:16]), waits=[t_v], inc="tAC")
        t_tb = op("te", lambda e: e.transpose(ps[5][0:8, 16:144], veh[:, :], IDENT),
                  waits=[t_ve], inc="tPE")
        t_fb = op("ac", lambda e: e.copy(te_b[:, :], ps[5][0:8, 16:144]),
                  waits=[t_tb, teb_war[0], teb_war[1]], inc="tAC")
        ps5war = t_fb
        t_s1 = dma("sy", vscr[cch, :].rearrange("(k two e) -> k two e", two=2, e=64)[:, 0:1, :],
                   te_b[:, 0:64].unsqueeze(1), waits=[t_fb])
        t_s2 = dma("sy", vscr[cch, :].rearrange("(k two e) -> k two e", two=2, e=64)[:, 1:2, :],
                   te_b[:, 64:128].unsqueeze(1), waits=[t_fb])
        teb_war = (t_s1, t_s2)
        vsr_ld[cch] = (t_s1, t_s2)
    t_exp = op("ac", lambda e: e.activation(probsb[:, :], scores[:, :], AF.Exp),
               waits=scores_evac, inc="tAC")
    t_d1 = op("ve", lambda e: e.tensor_add(den[:, 0:512], probsb[:, 0:512], probsb[:, 512:1024]),
              waits=[t_exp], inc="tVE")
    t_d2 = op("ve", lambda e: e.tensor_add(tmp1[0:16, 0:512], probsb[:, 1024:1536], probsb[:, 1536:2048]),
              waits=[t_exp], inc="tVE")
    t_d3 = op("ve", lambda e: e.tensor_add(den[:, 512:1024], den[:, 0:512], tmp1[0:16, 0:512]),
              waits=[t_d1, t_d2], inc="tVE")
    t_rc = op("ve", lambda e: e.reciprocal(tmp1[0:16, 512:1024], den[:, 512:1024]),
              waits=[t_d3], inc="tVE")
    tk_dsc = None
    if dbg:
        tk_dsc = op("gp", lambda e: e.dma_start(out=d_sc, in_=scores[:, :]),
                    waits=scores_evac, inc="dG", amt=16)
    pt_toks = []
    for cch in range(4):
        pt_toks.append(op("ve", lambda e, cch=cch: e.tensor_mul(
            scores[:, 512 * cch:512 * (cch + 1)], probsb[:, 512 * cch:512 * (cch + 1)],
            tmp1[0:16, 512:1024]),
            waits=[t_rc, tk_dsc], inc="tVE"))
    probs_tm_ev = {}
    for cch in range(4):
        for tl in range(4):
            tau = 4 * cch + tl
            par = tau % 2
            t_tr = op("te", lambda e, cch=cch, tl=tl, par=par: e.transpose(
                ps[par][:, 0:16], scores[:, 512 * cch + 128 * tl:512 * cch + 128 * (tl + 1)],
                IDENT[0:16, 0:16]),
                waits=[pt_toks[cch], tr_war.get(par)], inc="tPE")
            t_ev = op("ac", lambda e, tau=tau, par=par: e.copy(
                probs_tm[:, 16 * tau:16 * (tau + 1)], ps[par][:, 0:16]),
                waits=[t_tr], inc="tAC")
            tr_war[par] = t_ev
            probs_tm_ev[tau] = t_ev
    tk_probs_done = probs_tm_ev[NTILE - 1]

    x1st = [None] * NTILE
    ln2st = [None] * NTILE
    vbwar = {2: gwar.get(2), 3: gwar.get(3)}
    x1war = {}
    ln2war = {}
    tmplast = None
    vb_pe = None
    for g in range(4):
        t_vld = dma("sy", vsr2[0:1, :], vscr[g, :].rearrange("(a b) -> a b", a=1),
                    waits=[vsr_ld[g][0], vsr_ld[g][1], vb_pe])
        t_vb0 = op("te", lambda e: e.matmul(ps[2][:, 0:512], ONES1, vsr2[0:1, 0:512],
                                            start=True, stop=True, skip_group_check=True),
                   waits=[t_vld, vbwar.get(2)], inc="tPE")
        t_vb1 = op("te", lambda e: e.matmul(ps[3][:, 0:512], ONES1, vsr2[0:1, 512:1024],
                                            start=True, stop=True, skip_group_check=True),
                   waits=[t_vld, vbwar.get(3)], inc="tPE")
        vb_pe = t_vb1
        for tl in range(4):
            tau = 4 * g + tl
            pex = lambda tau, h0: probs_tm[:, 16 * tau + h0:16 * tau + h0 + 8].unsqueeze(2).broadcast_to([128, 8, 64])
            t_t0 = op("ve", lambda e, tau=tau: e.tensor_mul(
                tmp1[:, 0:512].rearrange("p (h d) -> p h d", d=64), pex(tau, 0),
                ps[2][:, 0:512].rearrange("p (h d) -> p h d", d=64)),
                waits=[t_vb0, probs_tm_ev[tau]], inc="tVE")
            t_t1 = op("ve", lambda e, tau=tau: e.tensor_mul(
                tmp1[:, 512:1024].rearrange("p (h d) -> p h d", d=64), pex(tau, 8),
                ps[3][:, 0:512].rearrange("p (h d) -> p h d", d=64)),
                waits=[t_vb1, probs_tm_ev[tau]], inc="tVE")
            if tl == 3:
                vbwar[2] = t_t0
                vbwar[3] = t_t1
            xs = tau % 2
            t_x1 = op("ve", lambda e, tau=tau, xs=xs: e.tensor_add(x1_t(xs), xt_t(tau), tmp1[:, :]),
                      waits=[t_t0, t_t1, tk_probs_done] + list(x1war.get(xs, ())), inc="tVE")
            tmplast = t_x1
            x1st[tau] = dma("sy", x1my[128 * tau:128 * (tau + 1), :], x1_t(xs), waits=[t_x1])
            st = stats[:, 8 * (tau % 4):8 * (tau % 4) + 8]
            t_sq = op("ac", lambda e, xs=xs: e.activation(sq[:, :], x1_t(xs), AF.Square),
                      waits=[t_x1, sqwar], inc="tAC")
            t_s1 = op("ve", lambda e, st=st, xs=xs: e.reduce_sum(st[:, 0:1], x1_t(xs), axis=AX.X),
                      waits=[t_x1], inc="tVE")
            t_s2 = op("ve", lambda e, st=st: e.reduce_sum(st[:, 1:2], sq[:, :], axis=AX.X),
                      waits=[t_sq], inc="tVE")
            sqwar = t_s2
            t_mn = op("ve", lambda e, st=st: e.tensor_scalar_mul(st[:, 2:3], st[:, 0:1], 1.0 / E),
                      waits=[t_s1], inc="tVE")
            t_ms = op("ve", lambda e, st=st: e.tensor_mul(st[:, 4:5], st[:, 2:3], st[:, 2:3]),
                      waits=[t_mn], inc="tVE")
            t_vr = op("ve", lambda e, st=st: e.scalar_tensor_tensor(
                st[:, 5:6], st[:, 1:2], 1.0 / E, st[:, 4:5], ALU.mult, ALU.subtract),
                waits=[t_ms], inc="tVE")
            t_sd = op("ac", lambda e, st=st: e.activation(st[:, 6:7], st[:, 5:6], AF.Sqrt, bias=EPSC[0:128, :]),
                      waits=[t_vr], inc="tAC")
            t_rs = op("ve", lambda e, st=st: e.reciprocal(st[:, 3:4], st[:, 6:7]),
                      waits=[t_sd], inc="tVE")
            slot = tau % 2
            t_ln2 = op("ve", lambda e, st=st, xs=xs, slot=slot: e.tensor_scalar(
                ln2t[:, E * slot:E * (slot + 1)], x1_t(xs), st[:, 2:3], st[:, 3:4],
                ALU.subtract, ALU.mult),
                waits=[t_rs, ln2war.get(slot)], inc="tVE")
            ln2st[tau] = dma("sy", ln2my[128 * tau:128 * (tau + 1), :], ln2t[:, E * slot:E * (slot + 1)],
                             waits=[t_ln2])
            ln2war[slot] = ln2st[tau]
            x1war[xs] = (t_ln2, x1st[tau])
        lastx1 = x1war[(4 * g + 3) % 2][0]
        for half in range(2):
            k = 2 * g + half
            dma("gp", ew1[:, 4096 * k:4096 * (k + 1)].rearrange("p (s q) -> p s q", q=2048),
                ew1_d[:, 4096 * k:4096 * (k + 1)].rearrange("p (s q) -> p s q", q=2048),
                waits=[lastx1], sem="dG")
    tk_ew1 = ("dG", CNT["dG"])
    tk_ccl2 = op("gp", lambda e: e.collective_compute(
        "AllGather", ALU.bypass, replica_groups=[list(range(NCORE))], ins=[ln2my], outs=[ln2all]),
        waits=ln2st, inc="cL2")

    lg_ev = {}
    fmwar = {}
    hwar = {0: tr_war.get(0), 1: tr_war.get(1)}
    pexwar = {4: scores_evac[3], 5: ps5war}
    for gr in range(NRG):
        buf = gr % 2
        fmo = 8 * RG * buf
        t_lds = []
        for cch in range(8):
            t_lds.append(dma("sy", FMA[:, fmo + RG * cch:fmo + RG * (cch + 1)],
                             ln2my[RG * gr:RG * (gr + 1), 128 * cch:128 * (cch + 1)],
                             waits=[ln2st[2 * gr], ln2st[2 * gr + 1], fmwar.get(buf)],
                             transpose=True))
        hev = {}
        for ht in range(32):
            par = ht % 2
            tkm = None
            for k in range(8):
                w = [t_lds[k]]
                if k == 0:
                    w += [tk_rw1, hwar.get(par)]
                tkm = op("te", lambda e, ht=ht, k=k, par=par, fmo=fmo: e.matmul(
                    ps[par][:, 0:RG], WBA[:, HID * k + 128 * ht:HID * k + 128 * (ht + 1)],
                    FMA[:, fmo + RG * k:fmo + RG * (k + 1)],
                    start=(k == 0), stop=(k == 7), skip_group_check=True),
                    waits=w, inc="tPE" if k == 7 else None)
            t_ev = op("ac", lambda e, ht=ht, par=par: e.activation(
                HB[:, RG * ht:RG * (ht + 1)], ps[par][:, 0:RG], AF.Relu, bias=rb1[:, ht:ht + 1]),
                waits=[tkm, tk_rb1], inc="tAC")
            hwar[par] = t_ev
            hev[ht] = t_ev
        fmwar[buf] = hev[31]
        t_lg = None
        for k2 in range(32):
            w = [hev[k2]]
            if k2 == 0:
                w += [tk_rw2, pexwar.get(4)]
            t_lg = op("te", lambda e, k2=k2: e.matmul(
                ps[4][0:8, 0:RG], rw2[:, 8 * k2:8 * (k2 + 1)], HB[:, RG * k2:RG * (k2 + 1)],
                start=(k2 == 0), stop=(k2 == 31), skip_group_check=True),
                waits=w, inc="tPE" if k2 == 31 else None)
        t_lge = op("ac", lambda e: e.activation(lgsb[:, :], ps[4][0:8, 0:RG], AF.Identity, bias=rb2[:, 0:1]),
                   waits=[t_lg, tk_rb2], inc="tAC")
        pexwar[4] = t_lge
        for tl in range(2):
            tau = 2 * gr + tl
            t_tr = op("te", lambda e, tl=tl: e.transpose(ps[5][:, 0:8], lgsb[:, 128 * tl:128 * (tl + 1)],
                                                         IDENT[0:8, 0:8]),
                      waits=[t_lge, pexwar.get(5)], inc="tPE")
            t_ev = op("ac", lambda e, tau=tau: e.copy(ltT[:, 8 * tau:8 * (tau + 1)], ps[5][:, 0:8]),
                      waits=[t_tr], inc="tAC")
            pexwar[5] = t_ev
            lg_ev[tau] = t_ev
    tk_router_pe = ("tPE", CNT["tPE"])
    for k in range(8):
        dma("gp", WBA[:, 4096 * k:4096 * (k + 1)].rearrange("p (s q) -> p s q", q=2048),
            ew2_d[:, 4096 * k:4096 * (k + 1)].rearrange("p (s q) -> p s q", q=2048),
            waits=[tk_router_pe], sem="dG")
    tk_ew2 = ("dG", CNT["dG"])

    LGV = ltT[:, :].rearrange("p (t e) -> p t e", e=8)
    lgall = list(lg_ev.values())
    t_m1 = op("ve", lambda e: e.reduce_max(m1s[:, :], LGV, axis=AX.X), waits=lgall, inc="tVE")
    m1b = m1s[:, :].unsqueeze(2).broadcast_to([128, 16, 8])
    t_eq = op("ve", lambda e: e.tensor_tensor(a1[:, :].rearrange("p (t e) -> p t e", e=8), LGV, m1b, ALU.is_equal),
              waits=[t_m1], inc="tVE")
    t_tm = op("ve", lambda e: e.tensor_mul(a2[:, :], a1[:, :], REV8), waits=[t_eq], inc="tVE")
    t_r1 = op("ve", lambda e: e.reduce_max(w2s[:, :], a2[:, :].rearrange("p (t e) -> p t e", e=8), axis=AX.X),
              waits=[t_tm], inc="tVE")
    t_i1 = op("ve", lambda e: e.tensor_scalar(i1s[:, :], w2s[:, :], -1.0, 8.0, ALU.mult, ALU.add),
              waits=[t_r1], inc="tVE")
    i1b = i1s[:, :].unsqueeze(2).broadcast_to([128, 16, 8])
    t_a1 = op("ve", lambda e: e.tensor_tensor(a1[:, :].rearrange("p (t e) -> p t e", e=8),
                                              IOTA8.rearrange("p (t e) -> p t e", e=8), i1b, ALU.is_equal),
              waits=[t_i1], inc="tVE")
    t_l2 = op("ve", lambda e: e.scalar_tensor_tensor(bsum[:, :], a1[:, :], -1e9, ltT[:, :], ALU.mult, ALU.add),
              waits=[t_a1], inc="tVE")
    BSV = bsum[:, :].rearrange("p (t e) -> p t e", e=8)
    t_m2 = op("ve", lambda e: e.reduce_max(m2s[:, :], BSV, axis=AX.X), waits=[t_l2], inc="tVE")
    m2b = m2s[:, :].unsqueeze(2).broadcast_to([128, 16, 8])
    t_eq2 = op("ve", lambda e: e.tensor_tensor(a2[:, :].rearrange("p (t e) -> p t e", e=8), BSV, m2b, ALU.is_equal),
               waits=[t_m2], inc="tVE")
    t_tm2 = op("ve", lambda e: e.tensor_mul(bsum[:, :], a2[:, :], REV8), waits=[t_eq2], inc="tVE")
    t_r2 = op("ve", lambda e: e.reduce_max(w2s[:, :], bsum[:, :].rearrange("p (t e) -> p t e", e=8), axis=AX.X),
              waits=[t_tm2], inc="tVE")
    t_i2 = op("ve", lambda e: e.tensor_scalar(i2s[:, :], w2s[:, :], -1.0, 8.0, ALU.mult, ALU.add),
              waits=[t_r2], inc="tVE")
    i2b = i2s[:, :].unsqueeze(2).broadcast_to([128, 16, 8])
    t_a2 = op("ve", lambda e: e.tensor_tensor(a2[:, :].rearrange("p (t e) -> p t e", e=8),
                                              IOTA8.rearrange("p (t e) -> p t e", e=8), i2b, ALU.is_equal),
              waits=[t_i2], inc="tVE")
    t_df = op("ve", lambda e: e.tensor_sub(w1s[:, :], m2s[:, :], m1s[:, :]), waits=[t_m2], inc="tVE")
    t_sg = op("ac", lambda e: e.activation(p2s[:, :], w1s[:, :], AF.Sigmoid), waits=[t_df], inc="tAC")
    t_p1 = op("ve", lambda e: e.tensor_scalar(p1s[:, :], p2s[:, :], -1.0, 1.0, ALU.mult, ALU.add),
              waits=[t_sg], inc="tVE")
    t_b = op("ve", lambda e: e.tensor_add(bsum[:, :], a1[:, :], a2[:, :]), waits=[t_a1, t_a2], inc="tVE")

    t_pe_cols = []
    for t in range(16):
        tk = op("te", lambda e, t=t: e.matmul(ps[4][:, 8 * t:8 * (t + 1)], L128,
                                              bsum[:, 8 * t:8 * (t + 1)], start=(t == 0), stop=True, skip_group_check=True),
                waits=[t_b, pexwar.get(4)], inc="tPE")
        t_pe_cols.append(tk)
        op("te", lambda e, t=t: e.matmul(ps[5][0:16, 0:8], UCOLS[:, 16 * t:16 * (t + 1)],
                                         bsum[:, 8 * t:8 * (t + 1)], start=(t == 0), stop=(t == 15), skip_group_check=True),
           waits=[t_b, pexwar.get(5)], inc="tPE")
    tk_cs = ("tPE", CNT["tPE"])
    t_cse = op("ac", lambda e: e.copy(colsum[:, :], ps[5][0:16, 0:8]), waits=[tk_cs], inc="tAC")
    t_cnt = op("te", lambda e: e.matmul(ps[5][0:4, 8:16], C4, colsum[:, :], start=True, stop=True, skip_group_check=True),
               waits=[t_cse], inc="tPE")
    t_cnte = op("ac", lambda e: e.copy(cnt_sb[:, :], ps[5][0:4, 8:16]), waits=[t_cnt], inc="tAC")
    t_cst = dma("sy", cntb, cnt_sb[:, :], waits=[t_cnte])
    t_cc = op("gp", lambda e: e.collective_compute(
        "AllGather", ALU.bypass, replica_groups=[list(range(NCORE))], ins=[cntb], outs=[cntg]),
        waits=[t_cst], inc="cCN")
    t_cgl = dma("sy", cag[:, :], cntg, waits=[t_cc])
    t_bs = op("te", lambda e: e.matmul(ps[5][0:4, 16:24], WJ, cag[:, :], start=True, stop=True, skip_group_check=True),
              waits=[t_cgl], inc="tPE")
    t_bse = op("ac", lambda e: e.copy(base[:, :], ps[5][0:4, 16:24]), waits=[t_bs], inc="tAC")
    t_r1m = op("te", lambda e: e.matmul(ps[5][0:16, 24:32], E4, base[:, :], start=True, stop=False, skip_group_check=True),
               waits=[t_bse], inc="tPE")
    t_r2m = op("te", lambda e: e.matmul(ps[5][0:16, 24:32], LBLK, colsum[:, :], start=False, stop=True, skip_group_check=True),
               waits=[t_cse], inc="tPE")
    t_re = op("ac", lambda e: e.copy(rall[:, :], ps[5][0:16, 24:32]), waits=[t_r2m], inc="tAC")
    t_rst = dma("sy", rscr, rall[:, :], waits=[t_re])
    t_rld = dma("sy", rrow[0:1, :], rscr.rearrange("a b -> (a b)").rearrange("(x f) -> x f", x=1),
                waits=[t_rst])
    t_k1 = []
    for t in range(16):
        t_k1.append(op("te", lambda e, t=t: e.matmul(ps[4][:, 8 * t:8 * (t + 1)], ONES1,
                                                     rrow[0:1, 8 * t:8 * (t + 1)], start=False, stop=True, skip_group_check=True),
                       waits=[t_rld, t_pe_cols[t]], inc="tPE"))
    t_pf = op("ve", lambda e: e.tensor_copy(pfull[:, :], ps[4][:, 0:128]), waits=t_k1, inc="tVE")
    t_q1 = op("ve", lambda e: e.tensor_mul(bsum[:, :], a1[:, :], pfull[:, :]), waits=[t_pf], inc="tVE")
    t_po1 = op("ve", lambda e: e.reduce_sum(w1s[:, :], bsum[:, :].rearrange("p (t e) -> p t e", e=8), axis=AX.X),
               waits=[t_q1], inc="tVE")
    t_q2 = op("ve", lambda e: e.tensor_mul(bsum[:, :], a2[:, :], pfull[:, :]), waits=[t_po1], inc="tVE")
    t_po2 = op("ve", lambda e: e.reduce_sum(w2s[:, :], bsum[:, :].rearrange("p (t e) -> p t e", e=8), axis=AX.X),
               waits=[t_q2], inc="tVE")
    sc_toks = []
    du_toks = []
    of_toks = []
    for kk, (poss, idxs, prb, scs, ofs, dus, tpo) in enumerate((
            (w1s, i1s, p1s, sc1, of1, du1, t_po1), (w2s, i2s, p2s, sc2, of2, du2, t_po2))):
        kb = 512 * kk
        t_kp = op("ve", lambda e, poss=poss, kb=kb: e.tensor_scalar(
            tmp1[0:128, kb + 0:kb + 16], poss[:, :], float(CAP), None, ALU.is_lt), waits=[tpo], inc="tVE")
        t_sc = op("ve", lambda e, prb=prb, scs=scs, kb=kb: e.tensor_mul(
            scs[:, :], prb[:, :], tmp1[0:128, kb + 0:kb + 16]),
            waits=[t_kp, t_p1], inc="tVE")
        sc_toks.append(t_sc)
        t_pc = op("ve", lambda e, poss=poss, kb=kb: e.tensor_scalar_min(
            tmp1[0:128, kb + 16:kb + 32], poss[:, :], float(CAP - 1)),
            waits=[tpo], inc="tVE")
        t_g1 = op("ve", lambda e, kb=kb: e.tensor_scalar(
            tmp1[0:128, kb + 112:kb + 128], tmp1[0:128, kb + 16:kb + 32], 1024.0, None, ALU.is_ge),
            waits=[t_pc], inc="tVE")
        t_g2 = op("ve", lambda e, kb=kb: e.tensor_scalar(
            tmp1[0:128, kb + 128:kb + 144], tmp1[0:128, kb + 16:kb + 32], 2048.0, None, ALU.is_ge),
            waits=[t_pc], inc="tVE")
        t_g3 = op("ve", lambda e, kb=kb: e.tensor_scalar(
            tmp1[0:128, kb + 144:kb + 160], tmp1[0:128, kb + 16:kb + 32], 3072.0, None, ALU.is_ge),
            waits=[t_pc], inc="tVE")
        t_ga_ = op("ve", lambda e, kb=kb: e.tensor_add(
            tmp1[0:128, kb + 160:kb + 176], tmp1[0:128, kb + 112:kb + 128], tmp1[0:128, kb + 128:kb + 144]),
            waits=[t_g1, t_g2], inc="tVE")
        t_gb_ = op("ve", lambda e, kb=kb: e.tensor_add(
            tmp1[0:128, kb + 128:kb + 144], tmp1[0:128, kb + 160:kb + 176], tmp1[0:128, kb + 144:kb + 160]),
            waits=[t_ga_, t_g3], inc="tVE")
        t_of0 = op("ve", lambda e, idxs=idxs, kb=kb: e.scalar_tensor_tensor(
            tmp1[0:128, kb + 32:kb + 48], idxs[:, :], 1024.0, tmp1[0:128, kb + 16:kb + 32],
            ALU.mult, ALU.add),
            waits=[t_pc], inc="tVE")
        t_of = op("ve", lambda e, kb=kb: e.scalar_tensor_tensor(
            tmp1[0:128, kb + 32:kb + 48], tmp1[0:128, kb + 128:kb + 144], 7168.0,
            tmp1[0:128, kb + 32:kb + 48], ALU.mult, ALU.add),
            waits=[t_gb_, t_of0], inc="tVE")
        t_ofu = op("ac", lambda e, ofs=ofs, kb=kb: e.copy(ofs[:, :], tmp1[0:128, kb + 32:kb + 48]),
                   waits=[t_of], inc="tAC")
        of_toks.append(t_ofu)
        t_d0 = op("ve", lambda e, idxs=idxs, poss=poss, kb=kb: e.scalar_tensor_tensor(
            tmp1[0:128, kb + 48:kb + 64], idxs[:, :], float(CAP), poss[:, :], ALU.mult, ALU.add),
            waits=[tpo], inc="tVE")
        t_b1 = op("ve", lambda e, kb=kb: e.tensor_sub(
            tmp1[0:128, kb + 64:kb + 80], tmp1[0:128, kb + 48:kb + 64], OOBC),
            waits=[t_d0], inc="tVE")
        t_b2 = op("ve", lambda e, kb=kb: e.tensor_mul(
            tmp1[0:128, kb + 80:kb + 96], tmp1[0:128, kb + 64:kb + 80], tmp1[0:128, kb + 0:kb + 16]),
            waits=[t_b1, t_kp], inc="tVE")
        t_b3 = op("ve", lambda e, kb=kb: e.tensor_add(
            tmp1[0:128, kb + 96:kb + 112], tmp1[0:128, kb + 80:kb + 96], OOBC),
            waits=[t_b2], inc="tVE")
        t_du = op("ac", lambda e, dus=dus, kb=kb: e.copy(dus[:, :], tmp1[0:128, kb + 96:kb + 112]),
                  waits=[t_b3], inc="tAC")
        du_toks.append(t_du)
    tk_z256 = op("ve", lambda e: e.memset(tmp1[:, 768:1024], 0.0), waits=[t_pf], inc="tVE")
    t_zt = dma("sy", tbl_d.rearrange("(p f) -> p f", p=128), tmp1[:, 768:1024], waits=[tk_z256])
    sc_dma = []
    for kk, dus in ((0, du1), (1, du2)):
        for t in range(16):
            sc_dma.append(op("gp", lambda e, dus=dus, t=t: e.indirect_dma_start(
                out=tbl_d.rearrange("(a b) -> a b", b=1),
                out_offset=IndirectOffsetOnAxis(dus[:, t:t + 1], axis=0),
                in_=VALB[:, t:t + 1], in_offset=None,
                bounds_check=REGS['b32k'], oob_is_err=False),
                waits=[du_toks[kk], t_zt, tk_cp], inc="dG", amt=16))
    t_rs_ = op("gp", lambda e: e.collective_compute(
        "ReduceScatter", ALU.add, replica_groups=[list(range(NCORE))],
        ins=[tbl_d.rearrange("(a b) -> a b", b=128)], outs=[tblrs]),
        waits=sc_dma, inc="cTB")

    t_tld = dma("sy", tblr[:, :], tblrs, waits=[t_rs_, ps5war])
    t_ttr = op("te", lambda e: e.transpose(ps[5][:, 0:32], tblr[:, :], IDENT[0:32, 0:32]),
               waits=[t_tld, t_re], inc="tPE")
    t_tte = op("ac", lambda e: e.copy(tblT[:, :], ps[5][:, 0:32]), waits=[t_ttr], inc="tAC")
    t_tm_ = op("ve", lambda e: e.tensor_scalar(tmsk[:, :], tblT[:, :], 0.5, None, ALU.is_lt),
               waits=[t_tte], inc="tVE")
    t_tg1 = op("ve", lambda e: e.scalar_tensor_tensor(idxg[:, :], tmsk[:, :], 1e6, tblT[:, :],
                                                      ALU.mult, ALU.add),
               waits=[t_tm_], inc="tVE")
    t_tg2 = op("ve", lambda e: e.tensor_scalar_add(tmsk[:, :], idxg[:, :], -1.0), waits=[t_tg1], inc="tVE")
    t_idxu = op("ac", lambda e: e.copy(idxu[:, :], tmsk[:, :]), waits=[t_tg2], inc="tAC")

    dtm = lambda r: LFB[:, 1024 * r:1024 * (r + 1)]
    etm = lambda r: LFB[:, 2048 + 1024 * r:2048 + 1024 * (r + 1)]
    efm = lambda c: LFB[:, 4096 + RG * c:4096 + RG * (c + 1)]
    dtm_war = {}
    etm_war = {}
    efm_war = {}
    dfm_war = {}
    hwar3 = {}
    eowar = {}
    dfm_of = {}
    eost_grp = {}
    cceo_n = 0

    def emit_chain(sg):
        buf = sg % 2
        fmo = 8 * RG * buf
        dtst = {}
        for half in range(2):
            s0 = RG * sg + 128 * half
            r = half
            t_ga = op("gp", lambda e, s0=s0, r=r: e.indirect_dma_start(
                out=dtm(r), out_offset=None, in_=ln2all,
                in_offset=IndirectOffsetOnAxis(idxu[:, s0 // 128:s0 // 128 + 1], axis=0),
                bounds_check=REGS['b16k'], oob_is_err=False),
                waits=[t_idxu, tk_ccl2, dtm_war.get(r)], inc="dG", amt=16)
            dtst[half] = dma("sy", dispD[s0:s0 + 128, :], dtm(r), waits=[t_ga])
            dtm_war[r] = dtst[half]
        t_dfm = []
        for cch in range(8):
            t_dfm.append(dma("sy", FMA[:, fmo + RG * cch:fmo + RG * (cch + 1)],
                             dispD[RG * sg:RG * (sg + 1), 128 * cch:128 * (cch + 1)],
                             waits=[dtst[0], dtst[1], dfm_war.get(buf)], transpose=True))
        dfm_of[sg] = t_dfm

    def emit_compute(sg):
        buf = sg % 2
        fmo = 8 * RG * buf
        t_dfm = dfm_of[sg]
        hev = {}
        for ht in range(32):
            par = ht % 2
            tkm = None
            for k in range(8):
                w = [t_dfm[k]]
                if k == 0:
                    w += [tk_ew1, hwar3.get(par, hwar.get(par))]
                tkm = op("te", lambda e, ht=ht, k=k, par=par, fmo=fmo: e.matmul(
                    ps[par][:, 0:RG], ew1[:, 4096 * k + 128 * ht:4096 * k + 128 * (ht + 1)],
                    FMA[:, fmo + RG * k:fmo + RG * (k + 1)],
                    start=(k == 0), stop=(k == 7), skip_group_check=True),
                    waits=w, inc="tPE" if k == 7 else None)
            t_ev = op("ac", lambda e, ht=ht, par=par: e.activation(
                HB[:, RG * ht:RG * (ht + 1)], ps[par][:, 0:RG], AF.Relu, bias=eb1[:, ht:ht + 1]),
                waits=[tkm, tk_eb1], inc="tAC")
            hwar3[par] = t_ev
            hev[ht] = t_ev
        dfm_war[buf] = hev[31]
        for ot in range(8):
            par = 2 + ot % 2
            tkm = None
            for k2 in range(32):
                w = [hev[k2]]
                if k2 == 0:
                    w += [tk_ew2, eowar.get(par, vbwar.get(par))]
                tkm = op("te", lambda e, ot=ot, k2=k2, par=par: e.matmul(
                    ps[par][:, 0:RG], WBA[:, 1024 * k2 + 128 * ot:1024 * k2 + 128 * (ot + 1)],
                    HB[:, RG * k2:RG * (k2 + 1)],
                    start=(k2 == 0), stop=(k2 == 31), skip_group_check=True),
                    waits=w, inc="tPE" if k2 == 31 else None)
            t_ev = op("ac", lambda e, ot=ot, par=par: e.activation(
                efm(ot), ps[par][:, 0:RG], AF.Identity, bias=eb2[:, ot:ot + 1]),
                waits=[tkm, tk_eb2, efm_war.get(ot)], inc="tAC")
            eowar[par] = t_ev
            t_st = dma("sy", eoD[128 * ot:128 * (ot + 1), RG * sg:RG * (sg + 1)], efm(ot),
                       waits=[t_ev])
            efm_war[ot] = t_st
        grp_st = [efm_war[ot] for ot in range(8)]
        es = []
        for half in range(2):
            s0 = RG * sg + 128 * half
            r = half
            t_tl = dma("sy", etm(r), eoD[:, s0:s0 + 128], waits=grp_st + [etm_war.get(r)],
                       transpose=True)
            t_es = dma("sy", eomy[s0:s0 + 128, :], etm(r), waits=[t_tl])
            etm_war[r] = t_es
            es.append(t_es)
        eost_grp[sg] = es

    for sg in range(NSG + 1):
        if sg < NSG:
            emit_chain(sg)
        if sg >= 1:
            emit_compute(sg - 1)
            done = sg - 1
            if done % 4 == 3:
                kc = done // 4
                w = [tok for g in range(4 * kc, 4 * kc + 4) for tok in eost_grp[g]]
                op("gp", lambda e, kc=kc: e.collective_compute(
                    "AllGather", ALU.bypass, replica_groups=[list(range(NCORE))],
                    ins=[eomy[1024 * kc:1024 * (kc + 1), :]],
                    outs=[eoall[8192 * kc:8192 * (kc + 1), :]]),
                    waits=w, inc="cEO")
                cceo_n += 1
    t_cceo = ("cEO", cceo_n)

    gb = lambda i, r: ew1[:, 4096 * i + 1024 * r:4096 * i + 1024 * (r + 1)]
    x1r = lambda r: AXA[:, 4096 + 1024 * r:4096 + 1024 * (r + 1)]
    yb = lambda r: AXA[:, 8192 + 1024 * r:8192 + 1024 * (r + 1)]
    ob = lambda r: AXA[:, 12288 + 1024 * r:12288 + 1024 * (r + 1)]
    g_war = {}
    y_war = {}
    o_war = {}
    out_st = []
    for tau in range(NTILE):
        r = tau % 2
        t_g0 = op("gp", lambda e, tau=tau, r=r: e.indirect_dma_start(
            out=gb(0, r), out_offset=None, in_=eoall,
            in_offset=IndirectOffsetOnAxis(of1[:, tau:tau + 1], axis=0),
            bounds_check=REGS['b32k'], oob_is_err=False),
            waits=[t_cceo, of_toks[0], g_war.get((0, r))], inc="dG", amt=16)
        t_g1 = op("gp", lambda e, tau=tau, r=r: e.indirect_dma_start(
            out=gb(1, r), out_offset=None, in_=eoall,
            in_offset=IndirectOffsetOnAxis(of2[:, tau:tau + 1], axis=0),
            bounds_check=REGS['b32k'], oob_is_err=False),
            waits=[t_cceo, of_toks[1], g_war.get((1, r))], inc="dG", amt=16)
        t_xl = dma("sy", x1r(r), x1my[128 * tau:128 * (tau + 1), :],
                   waits=[x1st[tau], t_cceo, o_war.get(r)])
        t_y0 = op("ve", lambda e, tau=tau, r=r: e.tensor_scalar_mul(yb(r), gb(0, r), sc1[:, tau:tau + 1]),
                  waits=[t_g0, sc_toks[0], y_war.get(r)], inc="tVE")
        t_y1 = op("ve", lambda e, tau=tau, r=r: e.scalar_tensor_tensor(
            yb(r), gb(1, r), sc2[:, tau:tau + 1], yb(r), ALU.mult, ALU.add),
            waits=[t_y0, t_g1, sc_toks[1]], inc="tVE")
        g_war[(0, r)] = t_y0
        g_war[(1, r)] = t_y1
        t_o = op("ve", lambda e, r=r: e.tensor_add(ob(r), yb(r), x1r(r)),
                 waits=[t_y1, t_xl], inc="tVE")
        y_war[r] = t_o
        t_os = dma("sy", out_d[128 * tau:128 * (tau + 1), :], ob(r), waits=[t_o])
        o_war[r] = t_os
        out_st.append(t_os)

    if dbg:
        op("gp", lambda e: e.dma_start(out=d_x1, in_=x1my), waits=x1st, inc="dG", amt=16)
        op("gp", lambda e: e.dma_start(out=d_ptm, in_=probs_tm[:, :]), waits=[tk_probs_done], inc="dG", amt=16)
        op("gp", lambda e: e.dma_start(out=d_vscr, in_=vscr), waits=[vsr_ld[3][0], vsr_ld[3][1]], inc="dG", amt=16)
        op("gp", lambda e: e.dma_start(out=d_xsr, in_=xsr), waits=[tk_ccxs], inc="dG", amt=16)
        op("gp", lambda e: e.dma_start(out=d_ln2, in_=ln2my), waits=ln2st, inc="dG", amt=16)
        op("gp", lambda e: e.dma_start(out=d_lg, in_=ltT[:, :]), waits=lgall, inc="dG", amt=16)
        op("gp", lambda e: e.dma_start(out=d_pos[:, 0:16], in_=w1s[:, :]), waits=[t_po1], inc="dG", amt=16)
        op("gp", lambda e: e.dma_start(out=d_pos[:, 16:32], in_=w2s[:, :]), waits=[t_po2], inc="dG", amt=16)
        op("gp", lambda e: e.dma_start(out=d_pos[:, 32:48], in_=sc1[:, :]), waits=[sc_toks[0]], inc="dG", amt=16)
        op("gp", lambda e: e.dma_start(out=d_pos[:, 48:64], in_=sc2[:, :]), waits=[sc_toks[1]], inc="dG", amt=16)
        op("gp", lambda e: e.dma_start(out=d_tbl, in_=tblrs), waits=[t_rs_], inc="dG", amt=16)
        op("gp", lambda e: e.dma_start(out=d_sm[0:16, 0:8], in_=colsum[:, :]), waits=[t_cse], inc="dG", amt=16)
        op("gp", lambda e: e.dma_start(out=d_sm[0:4, 8:16], in_=cnt_sb[:, :]), waits=[t_cnte], inc="dG", amt=16)
        op("gp", lambda e: e.dma_start(out=d_sm[0:32, 16:24], in_=cag[:, :]), waits=[t_cgl], inc="dG", amt=16)
        op("gp", lambda e: e.dma_start(out=d_sm[0:4, 24:32], in_=base[:, :]), waits=[t_bse], inc="dG", amt=16)
        op("gp", lambda e: e.dma_start(out=d_sm[0:16, 32:40], in_=rall[:, :]), waits=[t_re], inc="dG", amt=16)
        op("gp", lambda e: e.dma_start(out=d_sm[0:1, 40:168], in_=rrow[0:1, :]), waits=[t_rld], inc="dG", amt=16)
        op("gp", lambda e: e.dma_start(out=d_sm[:, 168:296], in_=pfull[:, :]), waits=[t_pf], inc="dG", amt=16)
        op("gp", lambda e: e.dma_start(out=d_eo, in_=eomy), waits=eo_st, inc="dG", amt=16)

    final_dS = ("dS", CNT["dS"])
    final_dG = ("dG", CNT["dG"])
    op("sy", lambda e: e.nop(), waits=[final_dS])
    op("gp", lambda e: e.nop(), waits=[final_dG, ("cEO", 1)])

    with nc.Block() as blk:
        def runner(code):
            def body(eng):
                if code == "gp":
                    REGS['b16k'] = eng.alloc_register("b16k")
                    eng.reg_mov(REGS['b16k'], NCORE * NTOK - 1)
                    REGS['b32k'] = eng.alloc_register("b32k")
                    eng.reg_mov(REGS['b32k'], NE * CAP - 1)
                seen = {}
                for (ec, waits, emit, inc) in EV:
                    if ec != code:
                        continue
                    for (s, v) in waits:
                        if seen.get(s, 0) < v:
                            eng.wait_ge(SEM[s], v)
                            seen[s] = v
                    inst = emit(eng)
                    if inc is not None:
                        inst.then_inc(SEM[inc[0]], inc[1])
            return body
        blk.vector(runner("ve"))
        blk.tensor(runner("te"))
        blk.scalar(runner("ac"))
        blk.sync(runner("sy"))
        blk.gpsimd(runner("gp"))
    ES.close()
    return nc


def kernel(L0_attn_x, L0_attn_Wq, L0_attn_Wk, L0_attn_Wv, L0_ffn_rw1, L0_ffn_rb1,
           L0_ffn_rw2, L0_ffn_rb2, L0_ffn_ew1, L0_ffn_eb1, L0_ffn_ew2, L0_ffn_eb2,
           _dbg=False):
    x = np.ascontiguousarray(np.asarray(L0_attn_x, np.float32))
    rw1 = np.asarray(L0_ffn_rw1, np.float32)
    rw1_r = np.ascontiguousarray(rw1.reshape(8, 128, HID).transpose(1, 0, 2).reshape(128, 8 * HID))
    rw2_r = np.ascontiguousarray(np.asarray(L0_ffn_rw2, np.float32).reshape(32, 128, NE)
                                 .transpose(1, 0, 2).reshape(128, 32 * NE))
    rb1_r = np.ascontiguousarray(np.asarray(L0_ffn_rb1, np.float32).reshape(32, 128).T)
    rb2_c = np.ascontiguousarray(np.asarray(L0_ffn_rb2, np.float32).reshape(8, 1))
    ew1 = np.asarray(L0_ffn_ew1, np.float32)
    ew2 = np.asarray(L0_ffn_ew2, np.float32)
    eb1 = np.asarray(L0_ffn_eb1, np.float32)
    eb2 = np.asarray(L0_ffn_eb2, np.float32)
    wq = np.ascontiguousarray(np.asarray(L0_attn_Wq, np.float32))
    wk = np.ascontiguousarray(np.asarray(L0_attn_Wk, np.float32))
    wv = np.ascontiguousarray(np.asarray(L0_attn_Wv, np.float32))

    nc = build(dbg=_dbg)
    in_maps = []
    for j in range(NCORE):
        xj = np.ascontiguousarray(x[:, TPC * j:TPC * (j + 1), :].reshape(NTOK, E))
        ew1_r = np.ascontiguousarray(ew1[j].reshape(8, 128, HID).transpose(1, 0, 2).reshape(128, 8 * HID))
        ew2_r = np.ascontiguousarray(ew2[j].reshape(32, 128, E).transpose(1, 0, 2).reshape(128, 32 * E))
        eb1_r = np.ascontiguousarray(eb1[j].reshape(32, 128).T)
        eb2_r = np.ascontiguousarray(eb2[j].reshape(8, 128).T)
        in_maps.append(dict(
            x_my=xj, wq=wq, wk=wk, wv=wv,
            rw1r=rw1_r, rw2r=rw2_r, rb1r=rb1_r, rb2c=rb2_c,
            ew1r=ew1_r, ew2r=ew2_r, eb1r=eb1_r, eb2r=eb2_r,
            cpk=make_constpack(j),
        ))
    res = run_bass_kernel_spmd(nc, in_maps, core_ids=list(range(NCORE)))
    out = np.empty((C, T, E), np.float32)
    for j in range(NCORE):
        out[:, TPC * j:TPC * (j + 1), :] = res.results[j]["out"].reshape(C, TPC, E)
    if _dbg:
        return out, res
    return out


# version 09
# speedup vs baseline: 1.0660x; 1.0660x over previous
import numpy as np
import ml_dtypes
from contextlib import ExitStack
from collections import defaultdict

import concourse.bass as bass
import concourse.mybir as mybir
from concourse.bass import IndirectOffsetOnAxis
from concourse.bass_utils import run_bass_kernel_spmd

F32 = mybir.dt.float32
BF16 = mybir.dt.bfloat16
U32 = mybir.dt.uint32
AF = mybir.ActivationFunctionType
ALU = mybir.AluOpType
AX = mybir.AxisListType
BF = ml_dtypes.bfloat16

NCORE = 8
C, T, E = 4, 4096, 1024
H, D = 16, 64
NE, KTOP, CAP, HID = 8, 2, 4096, 4096
TPC = T // NCORE
NTOK = C * TPC
NTILE = NTOK // 128
RG = 256
NRG = NTOK // RG
NSG = CAP // RG
EPS = 1e-5

_o = 0
def _blk(w):
    global _o
    s = _o; _o += w; return s
CO_IDENT = _blk(128)
CO_L128 = _blk(128)
CO_IOTA8 = _blk(128)
CO_REV8 = _blk(128)
CO_E16 = _blk(128)
CO_UCOLS = _blk(256)
CO_OOB = _blk(16)
CO_VALB = _blk(16)
CO_C4 = _blk(4)
CO_E4 = _blk(16)
CO_LBLK = _blk(16)
CO_WJ = _blk(4)
CO_ONES = _blk(128)
CO_EPS = _blk(1)
CO_I32 = _blk(1)
CPW_SB = _o
CO_ZERO = _blk(128)
CPW = _o


def make_constpack(j):
    cp = np.zeros((128, CPW), np.float32)
    cp[:, CO_IDENT:CO_IDENT + 128] = np.eye(128, dtype=np.float32)
    L = np.zeros((128, 128), np.float32)
    for i in range(128):
        L[:i, i] = 1.0
    cp[:, CO_L128:CO_L128 + 128] = L
    col = np.arange(128)
    cp[:, CO_IOTA8:CO_IOTA8 + 128] = np.tile((col % 8).astype(np.float32), (128, 1))
    cp[:, CO_REV8:CO_REV8 + 128] = np.tile((8.0 - (col % 8)).astype(np.float32), (128, 1))
    e16 = np.zeros((128, 128), np.float32)
    for c in range(8):
        e16[0:64, 16 * c + 2 * c] = 1.0
        e16[64:128, 16 * c + 2 * c + 1] = 1.0
    cp[:, CO_E16:CO_E16 + 128] = e16
    uc = np.zeros((128, 256), np.float32)
    for t in range(16):
        uc[:, 16 * t + t] = 1.0
    cp[:, CO_UCOLS:CO_UCOLS + 256] = uc
    cp[:, CO_OOB:CO_OOB + 16] = (40000 + np.arange(128)[:, None] * 16 + np.arange(16)[None, :]).astype(np.float32)
    cp[:, CO_VALB:CO_VALB + 16] = (j * NTOK + np.arange(16)[None, :] * 128 + np.arange(128)[:, None] + 1).astype(np.float32)
    c4 = np.zeros((128, 4), np.float32)
    for t in range(16):
        c4[t, t // 4] = 1.0
    cp[:, CO_C4:CO_C4 + 4] = c4
    e4 = np.zeros((128, 16), np.float32)
    for t in range(16):
        e4[t // 4, t] = 1.0
    cp[:, CO_E4:CO_E4 + 16] = e4
    lb = np.zeros((128, 16), np.float32)
    for t in range(16):
        for tp in range(16):
            if tp < t and tp // 4 == t // 4:
                lb[tp, t] = 1.0
    cp[:, CO_LBLK:CO_LBLK + 16] = lb
    wj = np.zeros((128, 4), np.float32)
    for jp in range(NCORE):
        for cprev in range(4):
            for c_ in range(4):
                if cprev < c_ or (cprev == c_ and jp < j):
                    wj[jp * 4 + cprev, c_] = 1.0
    cp[:, CO_WJ:CO_WJ + 4] = wj
    cp[:, CO_ONES:CO_ONES + 128] = 1.0
    cp[:, CO_EPS] = EPS
    cp[:, CO_I32] = 1.0 / 32.0
    return cp


def build(dbg=False):
    nc = bass.Bass("TRN2", target_bir_lowering=False)

    x_my = nc.dram_tensor("x_my", [NTOK, E], F32, kind="ExternalInput").ap()
    wq_d = nc.dram_tensor("wq", [D, D], F32, kind="ExternalInput").ap()
    wk_d = nc.dram_tensor("wk", [D, D], F32, kind="ExternalInput").ap()
    wv_d = nc.dram_tensor("wv", [D, D], F32, kind="ExternalInput").ap()
    rw1_d = nc.dram_tensor("rw1r", [128, 8 * HID], F32, kind="ExternalInput").ap()
    rw2_d = nc.dram_tensor("rw2r", [128, 32 * NE], F32, kind="ExternalInput").ap()
    ew1_d = nc.dram_tensor("ew1r", [128, 8 * HID], F32, kind="ExternalInput").ap()
    ew2_d = nc.dram_tensor("ew2r", [128, 32 * E], F32, kind="ExternalInput").ap()
    rb1_d = nc.dram_tensor("rb1r", [128, 32], F32, kind="ExternalInput").ap()
    eb1_d = nc.dram_tensor("eb1r", [128, 32], F32, kind="ExternalInput").ap()
    eb2_d = nc.dram_tensor("eb2r", [128, 8], F32, kind="ExternalInput").ap()
    rb2_d = nc.dram_tensor("rb2c", [8, 1], F32, kind="ExternalInput").ap()
    cp_d = nc.dram_tensor("cpk", [128, CPW], F32, kind="ExternalInput").ap()

    out_d = nc.dram_tensor("out", [NTOK, E], F32, kind="ExternalOutput").ap()

    ln2my = nc.dram_tensor("ln2my", [NTOK, E], BF16).ap()
    ln2all = nc.dram_tensor("ln2all", [NCORE * NTOK, E], BF16).ap()
    x1my = nc.dram_tensor("x1my", [NTOK, E], F32).ap()
    xsb = nc.dram_tensor("xsb", [4, E], F32).ap()
    xsr = nc.dram_tensor("xsr", [4, E], F32).ap()
    vscr = nc.dram_tensor("vscr", [4, E], F32).ap()
    mscr = nc.dram_tensor("mscr", [64, 64], F32).ap()
    cntb = nc.dram_tensor("cntb", [4, NE], F32).ap()
    cntg = nc.dram_tensor("cntg", [4 * NCORE, NE], F32).ap()
    rscr = nc.dram_tensor("rscr", [16, NE], F32).ap()
    tbl_d = nc.dram_tensor("tbl", [NE * CAP], F32).ap()
    tblrs = nc.dram_tensor("tblrs", [32, 128], F32).ap()
    dispD = nc.dram_tensor("dispD", [CAP, E], BF16).ap()
    eoD = nc.dram_tensor("eoD", [E, CAP], BF16).ap()
    eomy = nc.dram_tensor("eomy", [CAP, E], BF16).ap()
    eoall = nc.dram_tensor("eoall", [NCORE * CAP, E], BF16).ap()
    if dbg:
        d_sc = nc.dram_tensor("d_sc", [16, 2048], F32, kind="ExternalOutput").ap()
        d_sm = nc.dram_tensor("d_sm", [128, 300], F32, kind="ExternalOutput").ap()
        d_ptm = nc.dram_tensor("d_ptm", [128, 256], F32, kind="ExternalOutput").ap()
        d_vscr = nc.dram_tensor("d_vscr", [4, E], F32, kind="ExternalOutput").ap()
        d_xsr = nc.dram_tensor("d_xsr", [4, E], F32, kind="ExternalOutput").ap()
        d_x1 = nc.dram_tensor("d_x1", [NTOK, E], F32, kind="ExternalOutput").ap()
        d_ln2 = nc.dram_tensor("d_ln2", [NTOK, E], BF16, kind="ExternalOutput").ap()
        d_lg = nc.dram_tensor("d_lg", [128, 128], F32, kind="ExternalOutput").ap()
        d_pos = nc.dram_tensor("d_pos", [128, 64], F32, kind="ExternalOutput").ap()
        d_tbl = nc.dram_tensor("d_tbl", [32, 128], F32, kind="ExternalOutput").ap()
        d_eo = nc.dram_tensor("d_eo", [CAP, E], BF16, kind="ExternalOutput").ap()

    ES = ExitStack()
    def sb(name, shape, dt):
        return ES.enter_context(nc.sbuf_tensor(name, shape, dt))

    cpk = sb("cpk_sb", [128, CPW_SB], F32)
    IDENT = cpk[:, CO_IDENT:CO_IDENT + 128]
    L128 = cpk[:, CO_L128:CO_L128 + 128]
    IOTA8 = cpk[:, CO_IOTA8:CO_IOTA8 + 128]
    REV8 = cpk[:, CO_REV8:CO_REV8 + 128]
    E16 = cpk[:, CO_E16:CO_E16 + 128]
    UCOLS = cpk[:, CO_UCOLS:CO_UCOLS + 256]
    OOBC = cpk[:, CO_OOB:CO_OOB + 16]
    VALB = cpk[:, CO_VALB:CO_VALB + 16]
    C4 = cpk[0:16, CO_C4:CO_C4 + 4]
    E4 = cpk[0:4, CO_E4:CO_E4 + 16]
    LBLK = cpk[0:16, CO_LBLK:CO_LBLK + 16]
    WJ = cpk[0:32, CO_WJ:CO_WJ + 4]
    ONES1 = cpk[0:1, CO_ONES:CO_ONES + 128]
    EPSC = cpk[:, CO_EPS:CO_EPS + 1]
    I32C = cpk[:, CO_I32:CO_I32 + 1]

    wq = sb("wq_sb", [128, 64], F32)
    wk = sb("wk_sb", [128, 64], F32)
    wv = sb("wv_sb", [64, 64], F32)
    wvT = sb("wvT_sb", [64, 64], F32)
    wvblk = sb("wvblk_sb", [128, 128], F32)
    m_sb = sb("m_sb", [64, 64], F32)
    mblk = sb("mblk_sb", [128, 128], F32)
    rb1 = sb("rb1_sb", [128, 32], F32)
    eb1 = sb("eb1_sb", [128, 32], F32)
    eb2 = sb("eb2_sb", [128, 8], F32)
    rb2 = sb("rb2_sb", [8, 1], F32)
    rw2 = sb("rw2_sb", [128, 32 * NE], BF16)

    AXA = sb("axa", [128, 16 * E], F32)
    xt_t = lambda t: AXA[:, E * t:E * (t + 1)]
    ew1 = AXA.bitcast(BF16)
    WBA = sb("wba", [128, 8 * HID], BF16)
    HB = sb("hb", [128, 32 * RG], BF16)
    HBf = HB.bitcast(F32)
    FMA = sb("fma", [128, 2 * 8 * RG], BF16)
    LFA = sb("lfa", [128, 8 * 512], F32)
    LFB = LFA.bitcast(BF16)
    PB = sb("pb", [128, 2 * E], F32)
    scores = PB[0:16, 0:2048]
    probsb = HBf[0:16, 0:2048]
    x1_t = lambda s: PB[:, E * s:E * (s + 1)]
    ln1c = sb("ln1c", [128, 2 * 128], F32)
    tmp1 = sb("tmp1", [128, E], F32)
    sq = sb("sq", [128, E], F32)
    ln2t = sb("ln2t", [128, 2 * E], BF16)
    stats = sb("stats", [128, 32], F32)
    probs_tm = sb("probs_tm", [128, 16 * 16], F32)
    vsr2 = sb("vsr2", [1, E], F32)
    xsum = sb("xsum", [128, 32], F32)
    xsT = sb("xsT", [32, 128], F32)
    xc = sb("xc", [128, 8], F32)
    veh = sb("veh", [128, 8], F32)
    te_b = sb("te_b", [8, 128], F32)
    den = HBf[0:16, 2048:3072]
    ltT = sb("ltT", [128, 128], F32)
    a1 = sb("a1", [128, 128], F32)
    a2 = sb("a2", [128, 128], F32)
    bsum = sb("bsum", [128, 128], F32)
    pfull = sb("pfull", [128, 128], F32)
    m1s = sb("m1s", [128, 16], F32)
    m2s = sb("m2s", [128, 16], F32)
    i1s = sb("i1s", [128, 16], F32)
    i2s = sb("i2s", [128, 16], F32)
    p1s = sb("p1s", [128, 16], F32)
    p2s = sb("p2s", [128, 16], F32)
    w1s = sb("w1s", [128, 16], F32)
    w2s = sb("w2s", [128, 16], F32)
    sc1 = sb("sc1", [128, 16], F32)
    sc2 = sb("sc2", [128, 16], F32)
    of1 = sb("of1", [128, 16], U32)
    of2 = sb("of2", [128, 16], U32)
    du1 = sb("du1", [128, 16], U32)
    du2 = sb("du2", [128, 16], U32)
    colsum = sb("colsum", [16, 8], F32)
    cnt_sb = sb("cnt_sb", [4, 8], F32)
    cag = sb("cag", [32, 8], F32)
    base = sb("base", [4, 8], F32)
    rall = sb("rall", [16, 8], F32)
    rrow = sb("rrow", [1, 128], F32)
    lgsb = sb("lgsb", [8, RG], F32)
    prod = FMA.bitcast(F32)[:, 0:1024]
    tblT = tmp1[:, 640:672]
    tmsk = tmp1[:, 672:704]
    idxg = tmp1[:, 704:736]
    idxu = sb("idxu", [128, 32], U32)
    tblr = ln1c[0:32, 0:128]

    ps = [ES.enter_context(nc.psum_tensor(f"ps{i}", [128, 512], F32)) for i in range(8)]

    sem_names = ["dS", "dG", "tPE", "tAC", "tVE", "cXS", "cCN", "cL2", "cTB", "cEO"]
    SEM = {n: ES.enter_context(nc.semaphore(n)) for n in sem_names}

    EV = []
    CNT = defaultdict(int)
    REGS = {}

    def op(eng, emit, waits=(), inc=None, amt=1):
        tok = None
        if inc is not None:
            CNT[inc] += amt
            tok = (inc, CNT[inc])
        EV.append((eng, tuple(w for w in waits if w is not None), emit, (inc, amt) if inc else None))
        return tok

    def dma(eng_code, out, in_, waits=(), sem="dS", **kw):
        return op(eng_code, lambda e: e.dma_start(out=out, in_=in_, **kw), waits, inc=sem, amt=16)

    tk_cp = dma("gp", cpk[:, :], cp_d[:, 0:CPW_SB], sem="dG")
    tk_zq = dma("sy", wq[:, :], cp_d[:, CO_ZERO:CO_ZERO + 64])
    tk_zk = dma("sy", wk[:, :], cp_d[:, CO_ZERO:CO_ZERO + 64])
    tk_wq = dma("sy", wq[0:64, :], wq_d, waits=[tk_zq])
    tk_wk = dma("sy", wk[0:64, :], wk_d, waits=[tk_zk])
    tk_wv = dma("gp", wv[:, :], wv_d, sem="dG")
    tk_rb1 = dma("gp", rb1[:, :], rb1_d, sem="dG")
    tk_eb1 = dma("gp", eb1[:, :], eb1_d, sem="dG")
    tk_eb2 = dma("gp", eb2[:, :], eb2_d, sem="dG")
    tk_rb2 = dma("gp", rb2[:, :], rb2_d, sem="dG")
    tk_rw2 = dma("gp", rw2[:, :], rw2_d, sem="dG")

    tk_m = op("te", lambda e: e.matmul(ps[6][0:64, 0:64], wq[:, :], wk[:, :], start=True, stop=True, skip_group_check=True),
              waits=[tk_wq, tk_wk], inc="tPE")
    tk_me = op("ac", lambda e: e.copy(m_sb[:, :], ps[6][0:64, 0:64]), waits=[tk_m], inc="tAC")
    tk_mz = dma("sy", mblk[:, :], cp_d[:, CO_ZERO:CO_ZERO + 128])
    tk_mb1 = op("ac", lambda e: e.copy(mblk[0:64, 0:64], m_sb[:, :]), waits=[tk_me, tk_mz], inc="tAC")
    tk_mst = dma("sy", mscr, m_sb[:, :], waits=[tk_me])
    tk_mb2 = dma("sy", mblk[64:128, 64:128], mscr, waits=[tk_mst, tk_mz])
    tk_wvt = op("te", lambda e: e.transpose(ps[7][0:64, 0:64], wv[:, :], IDENT[0:64, 0:64]),
                waits=[tk_wv, tk_cp, tk_m], inc="tPE")
    tk_wvte1 = op("ac", lambda e: e.copy(wvT[:, :], ps[7][0:64, 0:64]), waits=[tk_wvt], inc="tAC")
    tk_zv = dma("sy", wvblk[:, :], cp_d[:, CO_ZERO:CO_ZERO + 128])
    tk_wb1 = op("ac", lambda e: e.copy(wvblk[0:64, 0:64], wvT[:, :]), waits=[tk_wvte1, tk_zv], inc="tAC")
    tk_wvst = dma("sy", mscr, wvT[:, :], waits=[tk_wvte1, tk_mb2])
    tk_wvld = dma("sy", wvblk[64:128, 64:128], mscr, waits=[tk_wvst, tk_zv])

    tk_xld = [None] * NTILE
    scores_evac = [None] * 4
    fm_last = {}
    tr_war = {}
    ln1war = {}
    gwar = {}
    prodwar = {}
    sqwar = None
    for g in range(4):
        e16_first_wait = scores_evac[g - 1] if g > 0 else None
        for tl in range(4):
            tau = 4 * g + tl
            tk_xld[tau] = dma("sy" if tau % 2 == 0 else "gp", xt_t(tau),
                              x_my[128 * tau:128 * (tau + 1), :],
                              sem="dS" if tau % 2 == 0 else "dG")
            st = stats[:, 8 * (tau % 4):8 * (tau % 4) + 8]
            t_sq = op("ac", lambda e, tau=tau: e.activation(sq[:, :], xt_t(tau), AF.Square),
                      waits=[tk_xld[tau], sqwar], inc="tAC")
            t_s1 = op("ve", lambda e, st=st, tau=tau: e.reduce_sum(st[:, 0:1], xt_t(tau), axis=AX.X),
                      waits=[tk_xld[tau]], inc="tVE")
            t_s2 = op("ve", lambda e, st=st: e.reduce_sum(st[:, 1:2], sq[:, :], axis=AX.X),
                      waits=[t_sq], inc="tVE")
            sqwar = t_s2
            t_mn = op("ve", lambda e, st=st: e.tensor_scalar_mul(st[:, 2:3], st[:, 0:1], 1.0 / E),
                      waits=[t_s1], inc="tVE")
            t_ms = op("ve", lambda e, st=st: e.tensor_mul(st[:, 4:5], st[:, 2:3], st[:, 2:3]),
                      waits=[t_mn], inc="tVE")
            t_vr = op("ve", lambda e, st=st: e.scalar_tensor_tensor(
                st[:, 5:6], st[:, 1:2], 1.0 / E, st[:, 4:5], ALU.mult, ALU.subtract),
                waits=[t_ms], inc="tVE")
            t_sd = op("ac", lambda e, st=st: e.activation(st[:, 6:7], st[:, 5:6], AF.Sqrt, bias=EPSC[0:128, :]),
                      waits=[t_vr], inc="tAC")
            t_rs = op("ve", lambda e, st=st: e.reciprocal(st[:, 3:4], st[:, 6:7]),
                      waits=[t_sd], inc="tVE")
            for c in range(8):
                slot = c % 2
                t_ap = op("ve", lambda e, tau=tau, c=c, slot=slot, st=st: e.tensor_scalar(
                    ln1c[:, 128 * slot:128 * (slot + 1)], xt_t(tau)[:, 128 * c:128 * (c + 1)],
                    st[:, 2:3], st[:, 3:4], ALU.subtract, ALU.mult),
                    waits=[t_rs, ln1war.get(slot)], inc="tVE")
                par = c % 2
                t_tr = op("te", lambda e, par=par, slot=slot: e.transpose(
                    ps[par][:, 0:128], ln1c[:, 128 * slot:128 * (slot + 1)], IDENT),
                    waits=[t_ap, tr_war.get(par)], inc="tPE")
                ln1war[slot] = t_tr
                t_ev = op("ac", lambda e, c=c, tl=tl, par=par: e.copy(
                    LFA[:, 512 * c + 128 * tl:512 * c + 128 * (tl + 1)], ps[par][:, 0:128]),
                    waits=[t_tr], inc="tAC")
                tr_war[par] = t_ev
                if tl == 3:
                    fm_last[(g, c)] = t_ev
        for c in range(8):
            par = 2 + (c % 2)
            t_g = op("te", lambda e, c=c, par=par: e.matmul(
                ps[par][:, 0:512], mblk[:, :], LFA[:, 512 * c:512 * (c + 1)], start=True, stop=True, skip_group_check=True),
                waits=[fm_last[(g, c)], tk_mb1, tk_mb2, gwar.get(par)], inc="tPE")
            t_p = op("ve", lambda e, c=c, par=par: e.tensor_mul(
                prod[:, 512 * (c % 2):512 * (c % 2 + 1)], LFA[:, 512 * c:512 * (c + 1)], ps[par][:, 0:512]),
                waits=[t_g, prodwar.get(c % 2)], inc="tVE")
            gwar[par] = t_p
            t_e16 = op("te", lambda e, c=c: e.matmul(
                ps[4][0:16, 0:512], E16[:, 16 * c:16 * (c + 1)], prod[:, 512 * (c % 2):512 * (c % 2 + 1)],
                start=(c == 0), stop=(c == 7), skip_group_check=True),
                waits=[t_p] + ([e16_first_wait] if c == 0 else []), inc="tPE")
            prodwar[c % 2] = t_e16
            op("ve", lambda e, g=g, c=c: e.reduce_sum(
                xsum[:, 8 * g + c:8 * g + c + 1], LFA[:, 512 * c:512 * (c + 1)], axis=AX.X),
                waits=[fm_last[(g, c)]], inc="tVE")
        scores_evac[g] = op("ac", lambda e, g=g: e.activation(
            scores[:, 512 * g:512 * (g + 1)], ps[4][0:16, 0:512], AF.Copy, scale=I32C[0:16, :]),
            waits=[prodwar[1]], inc="tAC")
    tk_xsum_all = ("tVE", CNT["tVE"])
    tk_rw1k = [dma("gp", WBA[:, HID * k:HID * (k + 1)].rearrange("p (s q) -> p s q", q=2048),
                   rw1_d[:, HID * k:HID * (k + 1)].rearrange("p (s q) -> p s q", q=2048), sem="dG")
               for k in range(8)]

    t_xt = op("te", lambda e: e.transpose(ps[6][0:32, 0:128], xsum[:, :], IDENT),
              waits=[tk_xsum_all, tk_me], inc="tPE")
    t_xte = op("ac", lambda e: e.copy(xsT[:, :], ps[6][0:32, 0:128]), waits=[t_xt], inc="tAC")
    t_xst = dma("sy", xsb.rearrange("a (k p) -> (a k) p", p=128), xsT[:, :], waits=[t_xte])
    tk_ccxs = op("gp", lambda e: e.collective_compute(
        "AllReduce", ALU.add, replica_groups=[list(range(NCORE))], ins=[xsb], outs=[xsr]),
        waits=[t_xst], inc="cXS")
    t_exp = op("ac", lambda e: e.activation(probsb[:, :], scores[:, :], AF.Exp),
               waits=scores_evac, inc="tAC")
    t_d1 = op("ve", lambda e: e.tensor_add(den[:, 0:512], probsb[:, 0:512], probsb[:, 512:1024]),
              waits=[t_exp], inc="tVE")
    t_d2 = op("ve", lambda e: e.tensor_add(tmp1[0:16, 0:512], probsb[:, 1024:1536], probsb[:, 1536:2048]),
              waits=[t_exp], inc="tVE")
    t_d3 = op("ve", lambda e: e.tensor_add(den[:, 512:1024], den[:, 0:512], tmp1[0:16, 0:512]),
              waits=[t_d1, t_d2], inc="tVE")
    t_rc = op("ve", lambda e: e.reciprocal(tmp1[0:16, 512:1024], den[:, 512:1024]),
              waits=[t_d3], inc="tVE")
    tk_dsc = None
    if dbg:
        tk_dsc = op("gp", lambda e: e.dma_start(out=d_sc, in_=scores[:, :]),
                    waits=scores_evac, inc="dG", amt=16)
    pt_toks = []
    for cch in range(4):
        pt_toks.append(op("ve", lambda e, cch=cch: e.tensor_mul(
            scores[:, 512 * cch:512 * (cch + 1)], probsb[:, 512 * cch:512 * (cch + 1)],
            tmp1[0:16, 512:1024]),
            waits=[t_rc, tk_dsc], inc="tVE"))
    probs_tm_ev = {}
    for cch in range(4):
        for tl in range(4):
            tau = 4 * cch + tl
            par = tau % 2
            t_tr = op("te", lambda e, cch=cch, tl=tl, par=par: e.transpose(
                ps[par][:, 0:16], scores[:, 512 * cch + 128 * tl:512 * cch + 128 * (tl + 1)],
                IDENT[0:16, 0:16]),
                waits=[pt_toks[cch], tr_war.get(par)], inc="tPE")
            t_ev = op("ac", lambda e, tau=tau, par=par: e.copy(
                probs_tm[:, 16 * tau:16 * (tau + 1)], ps[par][:, 0:16]),
                waits=[t_tr], inc="tAC")
            tr_war[par] = t_ev
            probs_tm_ev[tau] = t_ev
    tk_probs_done = probs_tm_ev[NTILE - 1]
    vsr_ld = [None] * 4
    ps5war = None
    teb_war = (None, None)
    for cch in range(4):
        t_l = dma("sy", tblr[0:8, :], xsr[cch, :].rearrange("(k p) -> k p", p=128), waits=[tk_ccxs, ps5war])
        t_t = op("te", lambda e: e.transpose(ps[5][:, 0:8], tblr[0:8, :], IDENT[0:8, 0:8]),
                 waits=[t_l, ps5war], inc="tPE")
        t_e = op("ac", lambda e: e.copy(xc[:, :], ps[5][:, 0:8]), waits=[t_t], inc="tAC")
        t_v = op("te", lambda e: e.matmul(ps[5][:, 8:16], wvblk[:, :], xc[:, :],
                                          start=True, stop=True, skip_group_check=True),
                 waits=[t_e, tk_wb1, tk_wvld], inc="tPE")
        t_ve = op("ac", lambda e: e.copy(veh[:, :], ps[5][:, 8:16]), waits=[t_v], inc="tAC")
        t_tb = op("te", lambda e: e.transpose(ps[5][0:8, 16:144], veh[:, :], IDENT),
                  waits=[t_ve], inc="tPE")
        t_fb = op("ac", lambda e: e.copy(te_b[:, :], ps[5][0:8, 16:144]),
                  waits=[t_tb, teb_war[0], teb_war[1]], inc="tAC")
        ps5war = t_fb
        t_s1 = dma("sy", vscr[cch, :].rearrange("(k two e) -> k two e", two=2, e=64)[:, 0:1, :],
                   te_b[:, 0:64].unsqueeze(1), waits=[t_fb])
        t_s2 = dma("sy", vscr[cch, :].rearrange("(k two e) -> k two e", two=2, e=64)[:, 1:2, :],
                   te_b[:, 64:128].unsqueeze(1), waits=[t_fb])
        teb_war = (t_s1, t_s2)
        vsr_ld[cch] = (t_s1, t_s2)

    x1st = [None] * NTILE
    ln2st = [None] * NTILE
    vbwar = {2: gwar.get(2), 3: gwar.get(3), 6: t_xte, 7: tk_wvte1}
    x1war = {}
    ln2war = {}
    vbst = {}
    vb_pe = None
    tmplast = None
    lg_ev = {}
    fmwar = {}
    t_lds_of = {}
    hwar = {0: tr_war.get(0), 1: tr_war.get(1)}
    pexwar = {4: scores_evac[3], 5: ps5war}

    def emit_vb(g):
        nonlocal vb_pe
        b0, b1 = (2, 3) if g % 2 == 0 else (6, 7)
        t_vld = dma("sy", vsr2[0:1, :], vscr[g, :].rearrange("(a b) -> a b", a=1),
                    waits=[vsr_ld[g][0], vsr_ld[g][1], vb_pe])
        t_vb0 = op("te", lambda e, b0=b0: e.matmul(ps[b0][:, 0:512], ONES1, vsr2[0:1, 0:512],
                                                   start=True, stop=True, skip_group_check=True),
                   waits=[t_vld, vbwar.get(b0)], inc="tPE")
        t_vb1 = op("te", lambda e, b1=b1: e.matmul(ps[b1][:, 0:512], ONES1, vsr2[0:1, 512:1024],
                                                   start=True, stop=True, skip_group_check=True),
                   waits=[t_vld, vbwar.get(b1)], inc="tPE")
        vb_pe = t_vb1
        vbst[g] = (t_vb0, t_vb1, b0, b1)

    def emit_p1b_tiles(g):
        nonlocal sqwar, tmplast
        t_vb0, t_vb1, b0, b1 = vbst[g]
        for tl in range(4):
            tau = 4 * g + tl
            pex = lambda tau, h0: probs_tm[:, 16 * tau + h0:16 * tau + h0 + 8].unsqueeze(2).broadcast_to([128, 8, 64])
            t_t0 = op("ve", lambda e, tau=tau, b0=b0: e.tensor_mul(
                tmp1[:, 0:512].rearrange("p (h d) -> p h d", d=64), pex(tau, 0),
                ps[b0][:, 0:512].rearrange("p (h d) -> p h d", d=64)),
                waits=[t_vb0, probs_tm_ev[tau]], inc="tVE")
            t_t1 = op("ve", lambda e, tau=tau, b1=b1: e.tensor_mul(
                tmp1[:, 512:1024].rearrange("p (h d) -> p h d", d=64), pex(tau, 8),
                ps[b1][:, 0:512].rearrange("p (h d) -> p h d", d=64)),
                waits=[t_vb1, probs_tm_ev[tau]], inc="tVE")
            if tl == 3:
                vbwar[b0] = t_t0
                vbwar[b1] = t_t1
            xs = tau % 2
            t_x1 = op("ve", lambda e, tau=tau, xs=xs: e.tensor_add(x1_t(xs), xt_t(tau), tmp1[:, :]),
                      waits=[t_t0, t_t1, tk_probs_done] + list(x1war.get(xs, ())), inc="tVE")
            tmplast = t_x1
            x1st[tau] = dma("sy", x1my[128 * tau:128 * (tau + 1), :], x1_t(xs), waits=[t_x1])
            st = stats[:, 8 * (tau % 4):8 * (tau % 4) + 8]
            t_sq = op("ac", lambda e, xs=xs: e.activation(sq[:, :], x1_t(xs), AF.Square),
                      waits=[t_x1, sqwar], inc="tAC")
            t_s1 = op("ve", lambda e, st=st, xs=xs: e.reduce_sum(st[:, 0:1], x1_t(xs), axis=AX.X),
                      waits=[t_x1], inc="tVE")
            t_s2 = op("ve", lambda e, st=st: e.reduce_sum(st[:, 1:2], sq[:, :], axis=AX.X),
                      waits=[t_sq], inc="tVE")
            sqwar = t_s2
            t_mn = op("ve", lambda e, st=st: e.tensor_scalar_mul(st[:, 2:3], st[:, 0:1], 1.0 / E),
                      waits=[t_s1], inc="tVE")
            t_ms = op("ve", lambda e, st=st: e.tensor_mul(st[:, 4:5], st[:, 2:3], st[:, 2:3]),
                      waits=[t_mn], inc="tVE")
            t_vr = op("ve", lambda e, st=st: e.scalar_tensor_tensor(
                st[:, 5:6], st[:, 1:2], 1.0 / E, st[:, 4:5], ALU.mult, ALU.subtract),
                waits=[t_ms], inc="tVE")
            t_sd = op("ac", lambda e, st=st: e.activation(st[:, 6:7], st[:, 5:6], AF.Sqrt, bias=EPSC[0:128, :]),
                      waits=[t_vr], inc="tAC")
            t_rs = op("ve", lambda e, st=st: e.reciprocal(st[:, 3:4], st[:, 6:7]),
                      waits=[t_sd], inc="tVE")
            slot = tau % 2
            t_ln2 = op("ve", lambda e, st=st, xs=xs, slot=slot: e.tensor_scalar(
                ln2t[:, E * slot:E * (slot + 1)], x1_t(xs), st[:, 2:3], st[:, 3:4],
                ALU.subtract, ALU.mult),
                waits=[t_rs, ln2war.get(slot)], inc="tVE")
            ln2st[tau] = dma("sy", ln2my[128 * tau:128 * (tau + 1), :], ln2t[:, E * slot:E * (slot + 1)],
                             waits=[t_ln2])
            ln2war[slot] = ln2st[tau]
            x1war[xs] = (t_ln2, x1st[tau])

    def emit_p2_chain(gr):
        buf = gr % 2
        fmo = 8 * RG * buf
        t_lds = []
        for cch in range(8):
            t_lds.append(dma("sy", FMA[:, fmo + RG * cch:fmo + RG * (cch + 1)],
                             ln2my[RG * gr:RG * (gr + 1), 128 * cch:128 * (cch + 1)],
                             waits=[ln2st[2 * gr], ln2st[2 * gr + 1], fmwar.get(buf)],
                             transpose=True))
        t_lds_of[gr] = t_lds

    def emit_p2_compute(gr):
        buf = gr % 2
        fmo = 8 * RG * buf
        t_lds = t_lds_of[gr]
        hev = {}
        for ht in range(32):
            par = ht % 2
            tkm = None
            for k in range(8):
                w = [t_lds[k]]
                if ht == 0:
                    w += [tk_rw1k[k]]
                if k == 0:
                    w += [hwar.get(par)]
                tkm = op("te", lambda e, ht=ht, k=k, par=par, fmo=fmo: e.matmul(
                    ps[par][:, 0:RG], WBA[:, HID * k + 128 * ht:HID * k + 128 * (ht + 1)],
                    FMA[:, fmo + RG * k:fmo + RG * (k + 1)],
                    start=(k == 0), stop=(k == 7), skip_group_check=True),
                    waits=w, inc="tPE" if k == 7 else None)
            t_ev = op("ac", lambda e, ht=ht, par=par: e.activation(
                HB[:, RG * ht:RG * (ht + 1)], ps[par][:, 0:RG], AF.Relu, bias=rb1[:, ht:ht + 1]),
                waits=[tkm, tk_rb1], inc="tAC")
            hwar[par] = t_ev
            hev[ht] = t_ev
        fmwar[buf] = hev[31]
        t_lg = None
        for k2 in range(32):
            w = [hev[k2]]
            if k2 == 0:
                w += [tk_rw2, pexwar.get(4)]
            t_lg = op("te", lambda e, k2=k2: e.matmul(
                ps[4][0:8, 0:RG], rw2[:, 8 * k2:8 * (k2 + 1)], HB[:, RG * k2:RG * (k2 + 1)],
                start=(k2 == 0), stop=(k2 == 31), skip_group_check=True),
                waits=w, inc="tPE" if k2 == 31 else None)
        t_lge = op("ac", lambda e: e.activation(lgsb[:, :], ps[4][0:8, 0:RG], AF.Identity, bias=rb2[:, 0:1]),
                   waits=[t_lg, tk_rb2], inc="tAC")
        pexwar[4] = t_lge
        for tl in range(2):
            tau = 2 * gr + tl
            t_tr = op("te", lambda e, tl=tl: e.transpose(ps[5][:, 0:8], lgsb[:, 128 * tl:128 * (tl + 1)],
                                                         IDENT[0:8, 0:8]),
                      waits=[t_lge, pexwar.get(5)], inc="tPE")
            t_ev = op("ac", lambda e, tau=tau: e.copy(ltT[:, 8 * tau:8 * (tau + 1)], ps[5][:, 0:8]),
                      waits=[t_tr], inc="tAC")
            pexwar[5] = t_ev
            lg_ev[tau] = t_ev

    emit_vb(0)
    emit_vb(1)
    for g in range(4):
        emit_p1b_tiles(g)
        if g >= 1:
            emit_p2_compute(2 * g - 2)
            emit_p2_compute(2 * g - 1)
        if g < 2:
            emit_vb(g + 2)
        emit_p2_chain(2 * g)
        emit_p2_chain(2 * g + 1)
        lastx1 = x1war[(4 * g + 3) % 2][0]
        for half in range(2):
            k = 2 * g + half
            dma("gp", ew1[:, 4096 * k:4096 * (k + 1)].rearrange("p (s q) -> p s q", q=2048),
                ew1_d[:, 4096 * k:4096 * (k + 1)].rearrange("p (s q) -> p s q", q=2048),
                waits=[lastx1], sem="dG")
    tk_ew1 = ("dG", CNT["dG"])
    tk_ccl2 = op("gp", lambda e: e.collective_compute(
        "AllGather", ALU.bypass, replica_groups=[list(range(NCORE))], ins=[ln2my], outs=[ln2all]),
        waits=ln2st, inc="cL2")
    emit_p2_compute(6)
    emit_p2_compute(7)
    tk_router_pe = ("tPE", CNT["tPE"])
    for k in range(8):
        dma("gp", WBA[:, 4096 * k:4096 * (k + 1)].rearrange("p (s q) -> p s q", q=2048),
            ew2_d[:, 4096 * k:4096 * (k + 1)].rearrange("p (s q) -> p s q", q=2048),
            waits=[tk_router_pe], sem="dG")
    tk_ew2 = ("dG", CNT["dG"])

    LGV = ltT[:, :].rearrange("p (t e) -> p t e", e=8)
    lgall = list(lg_ev.values())
    t_m1 = op("ve", lambda e: e.reduce_max(m1s[:, :], LGV, axis=AX.X), waits=lgall, inc="tVE")
    m1b = m1s[:, :].unsqueeze(2).broadcast_to([128, 16, 8])
    t_eq = op("ve", lambda e: e.tensor_tensor(a1[:, :].rearrange("p (t e) -> p t e", e=8), LGV, m1b, ALU.is_equal),
              waits=[t_m1], inc="tVE")
    t_tm = op("ve", lambda e: e.tensor_mul(a2[:, :], a1[:, :], REV8), waits=[t_eq], inc="tVE")
    t_r1 = op("ve", lambda e: e.reduce_max(w2s[:, :], a2[:, :].rearrange("p (t e) -> p t e", e=8), axis=AX.X),
              waits=[t_tm], inc="tVE")
    t_i1 = op("ve", lambda e: e.tensor_scalar(i1s[:, :], w2s[:, :], -1.0, 8.0, ALU.mult, ALU.add),
              waits=[t_r1], inc="tVE")
    i1b = i1s[:, :].unsqueeze(2).broadcast_to([128, 16, 8])
    t_a1 = op("ve", lambda e: e.tensor_tensor(a1[:, :].rearrange("p (t e) -> p t e", e=8),
                                              IOTA8.rearrange("p (t e) -> p t e", e=8), i1b, ALU.is_equal),
              waits=[t_i1], inc="tVE")
    t_l2 = op("ve", lambda e: e.scalar_tensor_tensor(bsum[:, :], a1[:, :], -1e9, ltT[:, :], ALU.mult, ALU.add),
              waits=[t_a1], inc="tVE")
    BSV = bsum[:, :].rearrange("p (t e) -> p t e", e=8)
    t_m2 = op("ve", lambda e: e.reduce_max(m2s[:, :], BSV, axis=AX.X), waits=[t_l2], inc="tVE")
    m2b = m2s[:, :].unsqueeze(2).broadcast_to([128, 16, 8])
    t_eq2 = op("ve", lambda e: e.tensor_tensor(a2[:, :].rearrange("p (t e) -> p t e", e=8), BSV, m2b, ALU.is_equal),
               waits=[t_m2], inc="tVE")
    t_tm2 = op("ve", lambda e: e.tensor_mul(bsum[:, :], a2[:, :], REV8), waits=[t_eq2], inc="tVE")
    t_r2 = op("ve", lambda e: e.reduce_max(w2s[:, :], bsum[:, :].rearrange("p (t e) -> p t e", e=8), axis=AX.X),
              waits=[t_tm2], inc="tVE")
    t_i2 = op("ve", lambda e: e.tensor_scalar(i2s[:, :], w2s[:, :], -1.0, 8.0, ALU.mult, ALU.add),
              waits=[t_r2], inc="tVE")
    i2b = i2s[:, :].unsqueeze(2).broadcast_to([128, 16, 8])
    t_a2 = op("ve", lambda e: e.tensor_tensor(a2[:, :].rearrange("p (t e) -> p t e", e=8),
                                              IOTA8.rearrange("p (t e) -> p t e", e=8), i2b, ALU.is_equal),
              waits=[t_i2], inc="tVE")
    t_df = op("ve", lambda e: e.tensor_sub(w1s[:, :], m2s[:, :], m1s[:, :]), waits=[t_m2], inc="tVE")
    t_sg = op("ac", lambda e: e.activation(p2s[:, :], w1s[:, :], AF.Sigmoid), waits=[t_df], inc="tAC")
    t_p1 = op("ve", lambda e: e.tensor_scalar(p1s[:, :], p2s[:, :], -1.0, 1.0, ALU.mult, ALU.add),
              waits=[t_sg], inc="tVE")
    t_b = op("ve", lambda e: e.tensor_add(bsum[:, :], a1[:, :], a2[:, :]), waits=[t_a1, t_a2], inc="tVE")

    t_pe_cols = []
    for t in range(16):
        tk = op("te", lambda e, t=t: e.matmul(ps[4][:, 8 * t:8 * (t + 1)], L128,
                                              bsum[:, 8 * t:8 * (t + 1)], start=(t == 0), stop=True, skip_group_check=True),
                waits=[t_b, pexwar.get(4)], inc="tPE")
        t_pe_cols.append(tk)
        op("te", lambda e, t=t: e.matmul(ps[5][0:16, 0:8], UCOLS[:, 16 * t:16 * (t + 1)],
                                         bsum[:, 8 * t:8 * (t + 1)], start=(t == 0), stop=(t == 15), skip_group_check=True),
           waits=[t_b, pexwar.get(5)], inc="tPE")
    tk_cs = ("tPE", CNT["tPE"])
    t_cse = op("ac", lambda e: e.copy(colsum[:, :], ps[5][0:16, 0:8]), waits=[tk_cs], inc="tAC")
    t_cnt = op("te", lambda e: e.matmul(ps[5][0:4, 8:16], C4, colsum[:, :], start=True, stop=True, skip_group_check=True),
               waits=[t_cse], inc="tPE")
    t_cnte = op("ac", lambda e: e.copy(cnt_sb[:, :], ps[5][0:4, 8:16]), waits=[t_cnt], inc="tAC")
    t_cst = dma("sy", cntb, cnt_sb[:, :], waits=[t_cnte])
    t_cc = op("gp", lambda e: e.collective_compute(
        "AllGather", ALU.bypass, replica_groups=[list(range(NCORE))], ins=[cntb], outs=[cntg]),
        waits=[t_cst], inc="cCN")
    t_cgl = dma("sy", cag[:, :], cntg, waits=[t_cc])
    t_bs = op("te", lambda e: e.matmul(ps[5][0:4, 16:24], WJ, cag[:, :], start=True, stop=True, skip_group_check=True),
              waits=[t_cgl], inc="tPE")
    t_bse = op("ac", lambda e: e.copy(base[:, :], ps[5][0:4, 16:24]), waits=[t_bs], inc="tAC")
    t_r1m = op("te", lambda e: e.matmul(ps[5][0:16, 24:32], E4, base[:, :], start=True, stop=False, skip_group_check=True),
               waits=[t_bse], inc="tPE")
    t_r2m = op("te", lambda e: e.matmul(ps[5][0:16, 24:32], LBLK, colsum[:, :], start=False, stop=True, skip_group_check=True),
               waits=[t_cse], inc="tPE")
    t_re = op("ac", lambda e: e.copy(rall[:, :], ps[5][0:16, 24:32]), waits=[t_r2m], inc="tAC")
    t_rst = dma("sy", rscr, rall[:, :], waits=[t_re])
    t_rld = dma("sy", rrow[0:1, :], rscr.rearrange("a b -> (a b)").rearrange("(x f) -> x f", x=1),
                waits=[t_rst])
    t_k1 = []
    for t in range(16):
        t_k1.append(op("te", lambda e, t=t: e.matmul(ps[4][:, 8 * t:8 * (t + 1)], ONES1,
                                                     rrow[0:1, 8 * t:8 * (t + 1)], start=False, stop=True, skip_group_check=True),
                       waits=[t_rld, t_pe_cols[t]], inc="tPE"))
    t_pf = op("ve", lambda e: e.tensor_copy(pfull[:, :], ps[4][:, 0:128]), waits=t_k1, inc="tVE")
    t_q1 = op("ve", lambda e: e.tensor_mul(bsum[:, :], a1[:, :], pfull[:, :]), waits=[t_pf], inc="tVE")
    t_po1 = op("ve", lambda e: e.reduce_sum(w1s[:, :], bsum[:, :].rearrange("p (t e) -> p t e", e=8), axis=AX.X),
               waits=[t_q1], inc="tVE")
    t_q2 = op("ve", lambda e: e.tensor_mul(bsum[:, :], a2[:, :], pfull[:, :]), waits=[t_po1], inc="tVE")
    t_po2 = op("ve", lambda e: e.reduce_sum(w2s[:, :], bsum[:, :].rearrange("p (t e) -> p t e", e=8), axis=AX.X),
               waits=[t_q2], inc="tVE")
    sc_toks = []
    du_toks = []
    of_toks = []
    for kk, (poss, idxs, prb, scs, ofs, dus, tpo) in enumerate((
            (w1s, i1s, p1s, sc1, of1, du1, t_po1), (w2s, i2s, p2s, sc2, of2, du2, t_po2))):
        kb = 512 * kk
        t_kp = op("ve", lambda e, poss=poss, kb=kb: e.tensor_scalar(
            tmp1[0:128, kb + 0:kb + 16], poss[:, :], float(CAP), None, ALU.is_lt), waits=[tpo], inc="tVE")
        t_sc = op("ve", lambda e, prb=prb, scs=scs, kb=kb: e.tensor_mul(
            scs[:, :], prb[:, :], tmp1[0:128, kb + 0:kb + 16]),
            waits=[t_kp, t_p1], inc="tVE")
        sc_toks.append(t_sc)
        t_pc = op("ve", lambda e, poss=poss, kb=kb: e.tensor_scalar_min(
            tmp1[0:128, kb + 16:kb + 32], poss[:, :], float(CAP - 1)),
            waits=[tpo], inc="tVE")
        t_gs = []
        for i in range(1, 8):
            t_gs.append(op("ve", lambda e, kb=kb, i=i: e.tensor_scalar(
                tmp1[0:128, kb + 112 + 16 * (i - 1):kb + 128 + 16 * (i - 1)],
                tmp1[0:128, kb + 16:kb + 32], 512.0 * i, None, ALU.is_ge),
                waits=[t_pc], inc="tVE"))
        acc = [kb + 224, kb + 240]
        t_acc = op("ve", lambda e, kb=kb: e.tensor_add(
            tmp1[0:128, kb + 224:kb + 240], tmp1[0:128, kb + 112:kb + 128],
            tmp1[0:128, kb + 128:kb + 144]),
            waits=[t_gs[0], t_gs[1]], inc="tVE")
        cur = 0
        for i in range(3, 8):
            src, dst = acc[cur], acc[1 - cur]
            t_acc = op("ve", lambda e, i=i, kb=kb, src=src, dst=dst: e.tensor_add(
                tmp1[0:128, dst:dst + 16], tmp1[0:128, src:src + 16],
                tmp1[0:128, kb + 112 + 16 * (i - 1):kb + 128 + 16 * (i - 1)]),
                waits=[t_acc, t_gs[i - 1]], inc="tVE")
            cur = 1 - cur
        accf = acc[cur]
        t_of0 = op("ve", lambda e, idxs=idxs, kb=kb: e.scalar_tensor_tensor(
            tmp1[0:128, kb + 32:kb + 48], idxs[:, :], 512.0, tmp1[0:128, kb + 16:kb + 32],
            ALU.mult, ALU.add),
            waits=[t_pc], inc="tVE")
        t_of = op("ve", lambda e, kb=kb, accf=accf: e.scalar_tensor_tensor(
            tmp1[0:128, kb + 32:kb + 48], tmp1[0:128, accf:accf + 16], 3584.0,
            tmp1[0:128, kb + 32:kb + 48], ALU.mult, ALU.add),
            waits=[t_acc, t_of0], inc="tVE")
        t_ofu = op("ac", lambda e, ofs=ofs, kb=kb: e.copy(ofs[:, :], tmp1[0:128, kb + 32:kb + 48]),
                   waits=[t_of], inc="tAC")
        of_toks.append(t_ofu)
        t_d0 = op("ve", lambda e, idxs=idxs, poss=poss, kb=kb: e.scalar_tensor_tensor(
            tmp1[0:128, kb + 48:kb + 64], idxs[:, :], float(CAP), poss[:, :], ALU.mult, ALU.add),
            waits=[tpo], inc="tVE")
        t_b1 = op("ve", lambda e, kb=kb: e.tensor_sub(
            tmp1[0:128, kb + 64:kb + 80], tmp1[0:128, kb + 48:kb + 64], OOBC),
            waits=[t_d0], inc="tVE")
        t_b2 = op("ve", lambda e, kb=kb: e.tensor_mul(
            tmp1[0:128, kb + 80:kb + 96], tmp1[0:128, kb + 64:kb + 80], tmp1[0:128, kb + 0:kb + 16]),
            waits=[t_b1, t_kp], inc="tVE")
        t_b3 = op("ve", lambda e, kb=kb: e.tensor_add(
            tmp1[0:128, kb + 96:kb + 112], tmp1[0:128, kb + 80:kb + 96], OOBC),
            waits=[t_b2], inc="tVE")
        t_du = op("ac", lambda e, dus=dus, kb=kb: e.copy(dus[:, :], tmp1[0:128, kb + 96:kb + 112]),
                  waits=[t_b3], inc="tAC")
        du_toks.append(t_du)
    tk_z256 = op("ve", lambda e: e.memset(tmp1[:, 768:1024], 0.0), waits=[t_pf], inc="tVE")
    t_zt = dma("sy", tbl_d.rearrange("(p f) -> p f", p=128), tmp1[:, 768:1024], waits=[tk_z256])
    sc_dma = []
    for kk, dus in ((0, du1), (1, du2)):
        for t in range(16):
            sc_dma.append(op("gp", lambda e, dus=dus, t=t: e.indirect_dma_start(
                out=tbl_d.rearrange("(a b) -> a b", b=1),
                out_offset=IndirectOffsetOnAxis(dus[:, t:t + 1], axis=0),
                in_=VALB[:, t:t + 1], in_offset=None,
                bounds_check=REGS['b32k'], oob_is_err=False),
                waits=[du_toks[kk], t_zt, tk_cp], inc="dG", amt=16))
    t_rs_ = op("gp", lambda e: e.collective_compute(
        "ReduceScatter", ALU.add, replica_groups=[list(range(NCORE))],
        ins=[tbl_d.rearrange("(a b) -> a b", b=128)], outs=[tblrs]),
        waits=sc_dma, inc="cTB")

    t_tld = dma("sy", tblr[:, :], tblrs, waits=[t_rs_, ps5war])
    t_ttr = op("te", lambda e: e.transpose(ps[5][:, 0:32], tblr[:, :], IDENT[0:32, 0:32]),
               waits=[t_tld, t_re], inc="tPE")
    t_tte = op("ac", lambda e: e.copy(tblT[:, :], ps[5][:, 0:32]), waits=[t_ttr], inc="tAC")
    t_tm_ = op("ve", lambda e: e.tensor_scalar(tmsk[:, :], tblT[:, :], 0.5, None, ALU.is_lt),
               waits=[t_tte], inc="tVE")
    t_tg1 = op("ve", lambda e: e.scalar_tensor_tensor(idxg[:, :], tmsk[:, :], 1e6, tblT[:, :],
                                                      ALU.mult, ALU.add),
               waits=[t_tm_], inc="tVE")
    t_tg2 = op("ve", lambda e: e.tensor_scalar_add(tmsk[:, :], idxg[:, :], -1.0), waits=[t_tg1], inc="tVE")
    t_idxu = op("ac", lambda e: e.copy(idxu[:, :], tmsk[:, :]), waits=[t_tg2], inc="tAC")

    dtm = lambda r: LFB[:, 1024 * r:1024 * (r + 1)]
    etm = lambda r: LFB[:, 2048 + 1024 * r:2048 + 1024 * (r + 1)]
    efm = lambda c: LFB[:, 4096 + RG * c:4096 + RG * (c + 1)]
    dtm_war = {}
    etm_war = {}
    efm_war = {}
    dfm_war = {}
    hwar3 = {}
    eowar = {}
    dfm_of = {}
    eost_grp = {}
    cceo_n = 0

    def emit_chain(sg):
        buf = sg % 2
        fmo = 8 * RG * buf
        dtst = {}
        for half in range(2):
            s0 = RG * sg + 128 * half
            r = half
            t_ga = op("gp", lambda e, s0=s0, r=r: e.indirect_dma_start(
                out=dtm(r), out_offset=None, in_=ln2all,
                in_offset=IndirectOffsetOnAxis(idxu[:, s0 // 128:s0 // 128 + 1], axis=0),
                bounds_check=REGS['b16k'], oob_is_err=False),
                waits=[t_idxu, tk_ccl2, dtm_war.get(r)], inc="dG", amt=16)
            dtst[half] = dma("sy", dispD[s0:s0 + 128, :], dtm(r), waits=[t_ga])
            dtm_war[r] = dtst[half]
        t_dfm = []
        for cch in range(8):
            t_dfm.append(dma("sy", FMA[:, fmo + RG * cch:fmo + RG * (cch + 1)],
                             dispD[RG * sg:RG * (sg + 1), 128 * cch:128 * (cch + 1)],
                             waits=[dtst[0], dtst[1], dfm_war.get(buf)], transpose=True))
        dfm_of[sg] = t_dfm

    def emit_compute(sg):
        buf = sg % 2
        fmo = 8 * RG * buf
        t_dfm = dfm_of[sg]
        hev = {}
        for ht in range(32):
            par = ht % 2
            tkm = None
            for k in range(8):
                w = [t_dfm[k]]
                if k == 0:
                    w += [tk_ew1, hwar3.get(par, hwar.get(par))]
                tkm = op("te", lambda e, ht=ht, k=k, par=par, fmo=fmo: e.matmul(
                    ps[par][:, 0:RG], ew1[:, 4096 * k + 128 * ht:4096 * k + 128 * (ht + 1)],
                    FMA[:, fmo + RG * k:fmo + RG * (k + 1)],
                    start=(k == 0), stop=(k == 7), skip_group_check=True),
                    waits=w, inc="tPE" if k == 7 else None)
            t_ev = op("ac", lambda e, ht=ht, par=par: e.activation(
                HB[:, RG * ht:RG * (ht + 1)], ps[par][:, 0:RG], AF.Relu, bias=eb1[:, ht:ht + 1]),
                waits=[tkm, tk_eb1], inc="tAC")
            hwar3[par] = t_ev
            hev[ht] = t_ev
        dfm_war[buf] = hev[31]
        for ot in range(8):
            par = 2 + ot % 2
            tkm = None
            for k2 in range(32):
                w = [hev[k2]]
                if k2 == 0:
                    w += [tk_ew2, eowar.get(par, vbwar.get(par))]
                tkm = op("te", lambda e, ot=ot, k2=k2, par=par: e.matmul(
                    ps[par][:, 0:RG], WBA[:, 1024 * k2 + 128 * ot:1024 * k2 + 128 * (ot + 1)],
                    HB[:, RG * k2:RG * (k2 + 1)],
                    start=(k2 == 0), stop=(k2 == 31), skip_group_check=True),
                    waits=w, inc="tPE" if k2 == 31 else None)
            t_ev = op("ac", lambda e, ot=ot, par=par: e.activation(
                efm(ot), ps[par][:, 0:RG], AF.Identity, bias=eb2[:, ot:ot + 1]),
                waits=[tkm, tk_eb2, efm_war.get(ot)], inc="tAC")
            eowar[par] = t_ev
            t_st = dma("sy", eoD[128 * ot:128 * (ot + 1), RG * sg:RG * (sg + 1)], efm(ot),
                       waits=[t_ev])
            efm_war[ot] = t_st
        grp_st = [efm_war[ot] for ot in range(8)]
        es = []
        for half in range(2):
            s0 = RG * sg + 128 * half
            r = half
            t_tl = dma("sy", etm(r), eoD[:, s0:s0 + 128], waits=grp_st + [etm_war.get(r)],
                       transpose=True)
            t_es = dma("sy", eomy[s0:s0 + 128, :], etm(r), waits=[t_tl])
            etm_war[r] = t_es
            es.append(t_es)
        eost_grp[sg] = es

    for sg in range(NSG + 1):
        if sg < NSG:
            emit_chain(sg)
        if sg >= 1:
            emit_compute(sg - 1)
            done = sg - 1
            if done % 2 == 1:
                kc = done // 2
                w = [tok for g in range(2 * kc, 2 * kc + 2) for tok in eost_grp[g]]
                op("gp", lambda e, kc=kc: e.collective_compute(
                    "AllGather", ALU.bypass, replica_groups=[list(range(NCORE))],
                    ins=[eomy[512 * kc:512 * (kc + 1), :]],
                    outs=[eoall[4096 * kc:4096 * (kc + 1), :]]),
                    waits=w, inc="cEO")
                cceo_n += 1
    t_cceo = ("cEO", cceo_n)

    gb = lambda i, r: ew1[:, 4096 * i + 1024 * r:4096 * i + 1024 * (r + 1)]
    x1r = lambda r: AXA[:, 4096 + 1024 * r:4096 + 1024 * (r + 1)]
    yb = lambda r: AXA[:, 8192 + 1024 * r:8192 + 1024 * (r + 1)]
    ob = lambda r: AXA[:, 12288 + 1024 * r:12288 + 1024 * (r + 1)]
    g_war = {}
    y_war = {}
    o_war = {}
    out_st = []
    for tau in range(NTILE):
        r = tau % 2
        t_g0 = op("gp", lambda e, tau=tau, r=r: e.indirect_dma_start(
            out=gb(0, r), out_offset=None, in_=eoall,
            in_offset=IndirectOffsetOnAxis(of1[:, tau:tau + 1], axis=0),
            bounds_check=REGS['b32k'], oob_is_err=False),
            waits=[t_cceo, of_toks[0], g_war.get((0, r))], inc="dG", amt=16)
        t_g1 = op("gp", lambda e, tau=tau, r=r: e.indirect_dma_start(
            out=gb(1, r), out_offset=None, in_=eoall,
            in_offset=IndirectOffsetOnAxis(of2[:, tau:tau + 1], axis=0),
            bounds_check=REGS['b32k'], oob_is_err=False),
            waits=[t_cceo, of_toks[1], g_war.get((1, r))], inc="dG", amt=16)
        t_xl = dma("sy", x1r(r), x1my[128 * tau:128 * (tau + 1), :],
                   waits=[x1st[tau], t_cceo, o_war.get(r)])
        t_y0 = op("ve", lambda e, tau=tau, r=r: e.tensor_scalar_mul(yb(r), gb(0, r), sc1[:, tau:tau + 1]),
                  waits=[t_g0, sc_toks[0], y_war.get(r)], inc="tVE")
        t_y1 = op("ve", lambda e, tau=tau, r=r: e.scalar_tensor_tensor(
            yb(r), gb(1, r), sc2[:, tau:tau + 1], yb(r), ALU.mult, ALU.add),
            waits=[t_y0, t_g1, sc_toks[1]], inc="tVE")
        g_war[(0, r)] = t_y0
        g_war[(1, r)] = t_y1
        t_o = op("ve", lambda e, r=r: e.tensor_add(ob(r), yb(r), x1r(r)),
                 waits=[t_y1, t_xl], inc="tVE")
        y_war[r] = t_o
        t_os = dma("sy", out_d[128 * tau:128 * (tau + 1), :], ob(r), waits=[t_o])
        o_war[r] = t_os
        out_st.append(t_os)

    if dbg:
        op("gp", lambda e: e.dma_start(out=d_x1, in_=x1my), waits=x1st, inc="dG", amt=16)
        op("gp", lambda e: e.dma_start(out=d_ptm, in_=probs_tm[:, :]), waits=[tk_probs_done], inc="dG", amt=16)
        op("gp", lambda e: e.dma_start(out=d_vscr, in_=vscr), waits=[vsr_ld[3][0], vsr_ld[3][1]], inc="dG", amt=16)
        op("gp", lambda e: e.dma_start(out=d_xsr, in_=xsr), waits=[tk_ccxs], inc="dG", amt=16)
        op("gp", lambda e: e.dma_start(out=d_ln2, in_=ln2my), waits=ln2st, inc="dG", amt=16)
        op("gp", lambda e: e.dma_start(out=d_lg, in_=ltT[:, :]), waits=lgall, inc="dG", amt=16)
        op("gp", lambda e: e.dma_start(out=d_pos[:, 0:16], in_=w1s[:, :]), waits=[t_po1], inc="dG", amt=16)
        op("gp", lambda e: e.dma_start(out=d_pos[:, 16:32], in_=w2s[:, :]), waits=[t_po2], inc="dG", amt=16)
        op("gp", lambda e: e.dma_start(out=d_pos[:, 32:48], in_=sc1[:, :]), waits=[sc_toks[0]], inc="dG", amt=16)
        op("gp", lambda e: e.dma_start(out=d_pos[:, 48:64], in_=sc2[:, :]), waits=[sc_toks[1]], inc="dG", amt=16)
        op("gp", lambda e: e.dma_start(out=d_tbl, in_=tblrs), waits=[t_rs_], inc="dG", amt=16)
        op("gp", lambda e: e.dma_start(out=d_sm[0:16, 0:8], in_=colsum[:, :]), waits=[t_cse], inc="dG", amt=16)
        op("gp", lambda e: e.dma_start(out=d_sm[0:4, 8:16], in_=cnt_sb[:, :]), waits=[t_cnte], inc="dG", amt=16)
        op("gp", lambda e: e.dma_start(out=d_sm[0:32, 16:24], in_=cag[:, :]), waits=[t_cgl], inc="dG", amt=16)
        op("gp", lambda e: e.dma_start(out=d_sm[0:4, 24:32], in_=base[:, :]), waits=[t_bse], inc="dG", amt=16)
        op("gp", lambda e: e.dma_start(out=d_sm[0:16, 32:40], in_=rall[:, :]), waits=[t_re], inc="dG", amt=16)
        op("gp", lambda e: e.dma_start(out=d_sm[0:1, 40:168], in_=rrow[0:1, :]), waits=[t_rld], inc="dG", amt=16)
        op("gp", lambda e: e.dma_start(out=d_sm[:, 168:296], in_=pfull[:, :]), waits=[t_pf], inc="dG", amt=16)
        op("gp", lambda e: e.dma_start(out=d_eo, in_=eomy), waits=eo_st, inc="dG", amt=16)

    final_dS = ("dS", CNT["dS"])
    final_dG = ("dG", CNT["dG"])
    op("sy", lambda e: e.nop(), waits=[final_dS])
    op("gp", lambda e: e.nop(), waits=[final_dG, ("cEO", 1)])

    with nc.Block() as blk:
        def runner(code):
            def body(eng):
                if code == "gp":
                    REGS['b16k'] = eng.alloc_register("b16k")
                    eng.reg_mov(REGS['b16k'], NCORE * NTOK - 1)
                    REGS['b32k'] = eng.alloc_register("b32k")
                    eng.reg_mov(REGS['b32k'], NE * CAP - 1)
                seen = {}
                for (ec, waits, emit, inc) in EV:
                    if ec != code:
                        continue
                    for (s, v) in waits:
                        if seen.get(s, 0) < v:
                            eng.wait_ge(SEM[s], v)
                            seen[s] = v
                    inst = emit(eng)
                    if inc is not None:
                        inst.then_inc(SEM[inc[0]], inc[1])
            return body
        blk.vector(runner("ve"))
        blk.tensor(runner("te"))
        blk.scalar(runner("ac"))
        blk.sync(runner("sy"))
        blk.gpsimd(runner("gp"))
    ES.close()
    return nc


def kernel(L0_attn_x, L0_attn_Wq, L0_attn_Wk, L0_attn_Wv, L0_ffn_rw1, L0_ffn_rb1,
           L0_ffn_rw2, L0_ffn_rb2, L0_ffn_ew1, L0_ffn_eb1, L0_ffn_ew2, L0_ffn_eb2,
           _dbg=False):
    x = np.ascontiguousarray(np.asarray(L0_attn_x, np.float32))
    rw1 = np.asarray(L0_ffn_rw1, np.float32)
    rw1_r = np.ascontiguousarray(rw1.reshape(8, 128, HID).transpose(1, 0, 2).reshape(128, 8 * HID))
    rw2_r = np.ascontiguousarray(np.asarray(L0_ffn_rw2, np.float32).reshape(32, 128, NE)
                                 .transpose(1, 0, 2).reshape(128, 32 * NE))
    rb1_r = np.ascontiguousarray(np.asarray(L0_ffn_rb1, np.float32).reshape(32, 128).T)
    rb2_c = np.ascontiguousarray(np.asarray(L0_ffn_rb2, np.float32).reshape(8, 1))
    ew1 = np.asarray(L0_ffn_ew1, np.float32)
    ew2 = np.asarray(L0_ffn_ew2, np.float32)
    eb1 = np.asarray(L0_ffn_eb1, np.float32)
    eb2 = np.asarray(L0_ffn_eb2, np.float32)
    wq = np.ascontiguousarray(np.asarray(L0_attn_Wq, np.float32))
    wk = np.ascontiguousarray(np.asarray(L0_attn_Wk, np.float32))
    wv = np.ascontiguousarray(np.asarray(L0_attn_Wv, np.float32))

    nc = build(dbg=_dbg)
    in_maps = []
    for j in range(NCORE):
        xj = np.ascontiguousarray(x[:, TPC * j:TPC * (j + 1), :].reshape(NTOK, E))
        ew1_r = np.ascontiguousarray(ew1[j].reshape(8, 128, HID).transpose(1, 0, 2).reshape(128, 8 * HID))
        ew2_r = np.ascontiguousarray(ew2[j].reshape(32, 128, E).transpose(1, 0, 2).reshape(128, 32 * E))
        eb1_r = np.ascontiguousarray(eb1[j].reshape(32, 128).T)
        eb2_r = np.ascontiguousarray(eb2[j].reshape(8, 128).T)
        in_maps.append(dict(
            x_my=xj, wq=wq, wk=wk, wv=wv,
            rw1r=rw1_r, rw2r=rw2_r, rb1r=rb1_r, rb2c=rb2_c,
            ew1r=ew1_r, ew2r=ew2_r, eb1r=eb1_r, eb2r=eb2_r,
            cpk=make_constpack(j),
        ))
    res = run_bass_kernel_spmd(nc, in_maps, core_ids=list(range(NCORE)))
    out = np.empty((C, T, E), np.float32)
    for j in range(NCORE):
        out[:, TPC * j:TPC * (j + 1), :] = res.results[j]["out"].reshape(C, TPC, E)
    if _dbg:
        return out, res
    return out


# version 11
# speedup vs baseline: 1.0799x; 1.0130x over previous
import numpy as np
import ml_dtypes
from contextlib import ExitStack
from collections import defaultdict

import concourse.bass as bass
import concourse.mybir as mybir
from concourse.bass import IndirectOffsetOnAxis
from concourse.bass_utils import run_bass_kernel_spmd

F32 = mybir.dt.float32
BF16 = mybir.dt.bfloat16
U32 = mybir.dt.uint32
AF = mybir.ActivationFunctionType
ALU = mybir.AluOpType
AX = mybir.AxisListType
BF = ml_dtypes.bfloat16

NCORE = 8
C, T, E = 4, 4096, 1024
H, D = 16, 64
NE, KTOP, CAP, HID = 8, 2, 4096, 4096
TPC = T // NCORE
NTOK = C * TPC
NTILE = NTOK // 128
RG = 256
NRG = NTOK // RG
NSG = CAP // RG
EPS = 1e-5

_o = 0
def _blk(w):
    global _o
    s = _o; _o += w; return s
CO_IDENT = _blk(128)
CO_L128 = _blk(128)
CO_IOTA8 = _blk(128)
CO_REV8 = _blk(128)
CO_E16 = _blk(128)
CO_UCOLS = _blk(256)
CO_OOB = _blk(16)
CO_VALB = _blk(16)
CO_C4 = _blk(4)
CO_E4 = _blk(16)
CO_LBLK = _blk(16)
CO_WJ = _blk(4)
CO_ONES = _blk(128)
CO_EPS = _blk(1)
CO_I32 = _blk(1)
CPW_SB = _o
CO_ZERO = _blk(128)
CPW = _o


def make_constpack(j):
    cp = np.zeros((128, CPW), np.float32)
    cp[:, CO_IDENT:CO_IDENT + 128] = np.eye(128, dtype=np.float32)
    L = np.zeros((128, 128), np.float32)
    for i in range(128):
        L[:i, i] = 1.0
    cp[:, CO_L128:CO_L128 + 128] = L
    col = np.arange(128)
    cp[:, CO_IOTA8:CO_IOTA8 + 128] = np.tile((col % 8).astype(np.float32), (128, 1))
    cp[:, CO_REV8:CO_REV8 + 128] = np.tile((8.0 - (col % 8)).astype(np.float32), (128, 1))
    e16 = np.zeros((128, 128), np.float32)
    for c in range(8):
        e16[0:64, 16 * c + 2 * c] = 1.0
        e16[64:128, 16 * c + 2 * c + 1] = 1.0
    cp[:, CO_E16:CO_E16 + 128] = e16
    uc = np.zeros((128, 256), np.float32)
    for t in range(16):
        uc[:, 16 * t + t] = 1.0
    cp[:, CO_UCOLS:CO_UCOLS + 256] = uc
    cp[:, CO_OOB:CO_OOB + 16] = (40000 + np.arange(128)[:, None] * 16 + np.arange(16)[None, :]).astype(np.float32)
    cp[:, CO_VALB:CO_VALB + 16] = (j * NTOK + np.arange(16)[None, :] * 128 + np.arange(128)[:, None] + 1).astype(np.float32)
    c4 = np.zeros((128, 4), np.float32)
    for t in range(16):
        c4[t, t // 4] = 1.0
    cp[:, CO_C4:CO_C4 + 4] = c4
    e4 = np.zeros((128, 16), np.float32)
    for t in range(16):
        e4[t // 4, t] = 1.0
    cp[:, CO_E4:CO_E4 + 16] = e4
    lb = np.zeros((128, 16), np.float32)
    for t in range(16):
        for tp in range(16):
            if tp < t and tp // 4 == t // 4:
                lb[tp, t] = 1.0
    cp[:, CO_LBLK:CO_LBLK + 16] = lb
    wj = np.zeros((128, 4), np.float32)
    for jp in range(NCORE):
        for cprev in range(4):
            for c_ in range(4):
                if cprev < c_ or (cprev == c_ and jp < j):
                    wj[jp * 4 + cprev, c_] = 1.0
    cp[:, CO_WJ:CO_WJ + 4] = wj
    cp[:, CO_ONES:CO_ONES + 128] = 1.0
    cp[:, CO_EPS] = EPS
    cp[:, CO_I32] = 1.0 / 32.0
    return cp


def build(dbg=False):
    nc = bass.Bass("TRN2", target_bir_lowering=False)

    x_my = nc.dram_tensor("x_my", [NTOK, E], F32, kind="ExternalInput").ap()
    wq_d = nc.dram_tensor("wq", [D, D], F32, kind="ExternalInput").ap()
    wk_d = nc.dram_tensor("wk", [D, D], F32, kind="ExternalInput").ap()
    wv_d = nc.dram_tensor("wv", [D, D], F32, kind="ExternalInput").ap()
    rw1_d = nc.dram_tensor("rw1r", [128, 8 * HID], F32, kind="ExternalInput").ap()
    rw2_d = nc.dram_tensor("rw2r", [128, 32 * NE], F32, kind="ExternalInput").ap()
    ew1_d = nc.dram_tensor("ew1r", [128, 8 * HID], F32, kind="ExternalInput").ap()
    ew2_d = nc.dram_tensor("ew2r", [128, 32 * E], F32, kind="ExternalInput").ap()
    rb1_d = nc.dram_tensor("rb1r", [128, 32], F32, kind="ExternalInput").ap()
    eb1_d = nc.dram_tensor("eb1r", [128, 32], F32, kind="ExternalInput").ap()
    eb2_d = nc.dram_tensor("eb2r", [128, 8], F32, kind="ExternalInput").ap()
    rb2_d = nc.dram_tensor("rb2c", [8, 1], F32, kind="ExternalInput").ap()
    cp_d = nc.dram_tensor("cpk", [128, CPW], F32, kind="ExternalInput").ap()

    out_d = nc.dram_tensor("out", [NTOK, E], F32, kind="ExternalOutput").ap()

    ln2my = nc.dram_tensor("ln2my", [NTOK, E], BF16).ap()
    ln2all = nc.dram_tensor("ln2all", [NCORE * NTOK, E], BF16).ap()
    x1my = nc.dram_tensor("x1my", [NTOK, E], F32).ap()
    xsb = nc.dram_tensor("xsb", [4, E], F32).ap()
    xsr = nc.dram_tensor("xsr", [4, E], F32).ap()
    vscr = nc.dram_tensor("vscr", [4, E], F32).ap()
    mscr = nc.dram_tensor("mscr", [64, 64], F32).ap()
    cntb = nc.dram_tensor("cntb", [4, NE], F32).ap()
    cntg = nc.dram_tensor("cntg", [4 * NCORE, NE], F32).ap()
    rscr = nc.dram_tensor("rscr", [16, NE], F32).ap()
    tbl_d = nc.dram_tensor("tbl", [NE * CAP], F32).ap()
    tblrs = nc.dram_tensor("tblrs", [32, 128], F32).ap()
    dispD = nc.dram_tensor("dispD", [CAP, E], BF16).ap()
    eoD = nc.dram_tensor("eoD", [E, CAP], BF16).ap()
    eomy = nc.dram_tensor("eomy", [CAP, E], BF16).ap()
    eoall = nc.dram_tensor("eoall", [NCORE * CAP, E], BF16).ap()
    if dbg:
        d_sc = nc.dram_tensor("d_sc", [16, 2048], F32, kind="ExternalOutput").ap()
        d_sm = nc.dram_tensor("d_sm", [128, 300], F32, kind="ExternalOutput").ap()
        d_ptm = nc.dram_tensor("d_ptm", [128, 256], F32, kind="ExternalOutput").ap()
        d_vscr = nc.dram_tensor("d_vscr", [4, E], F32, kind="ExternalOutput").ap()
        d_xsr = nc.dram_tensor("d_xsr", [4, E], F32, kind="ExternalOutput").ap()
        d_x1 = nc.dram_tensor("d_x1", [NTOK, E], F32, kind="ExternalOutput").ap()
        d_ln2 = nc.dram_tensor("d_ln2", [NTOK, E], BF16, kind="ExternalOutput").ap()
        d_lg = nc.dram_tensor("d_lg", [128, 128], F32, kind="ExternalOutput").ap()
        d_pos = nc.dram_tensor("d_pos", [128, 64], F32, kind="ExternalOutput").ap()
        d_tbl = nc.dram_tensor("d_tbl", [32, 128], F32, kind="ExternalOutput").ap()
        d_eo = nc.dram_tensor("d_eo", [CAP, E], BF16, kind="ExternalOutput").ap()

    ES = ExitStack()
    def sb(name, shape, dt):
        return ES.enter_context(nc.sbuf_tensor(name, shape, dt))

    cpk = sb("cpk_sb", [128, CPW_SB], F32)
    IDENT = cpk[:, CO_IDENT:CO_IDENT + 128]
    L128 = cpk[:, CO_L128:CO_L128 + 128]
    IOTA8 = cpk[:, CO_IOTA8:CO_IOTA8 + 128]
    REV8 = cpk[:, CO_REV8:CO_REV8 + 128]
    E16 = cpk[:, CO_E16:CO_E16 + 128]
    UCOLS = cpk[:, CO_UCOLS:CO_UCOLS + 256]
    OOBC = cpk[:, CO_OOB:CO_OOB + 16]
    VALB = cpk[:, CO_VALB:CO_VALB + 16]
    C4 = cpk[0:16, CO_C4:CO_C4 + 4]
    E4 = cpk[0:4, CO_E4:CO_E4 + 16]
    LBLK = cpk[0:16, CO_LBLK:CO_LBLK + 16]
    WJ = cpk[0:32, CO_WJ:CO_WJ + 4]
    ONES1 = cpk[0:1, CO_ONES:CO_ONES + 128]
    EPSC = cpk[:, CO_EPS:CO_EPS + 1]
    I32C = cpk[:, CO_I32:CO_I32 + 1]

    wq = sb("wq_sb", [128, 64], F32)
    wk = sb("wk_sb", [128, 64], F32)
    wv = sb("wv_sb", [64, 64], F32)
    wvT = sb("wvT_sb", [64, 64], F32)
    wvblk = sb("wvblk_sb", [128, 128], F32)
    m_sb = sb("m_sb", [64, 64], F32)
    mblk = sb("mblk_sb", [128, 128], F32)
    rb1 = sb("rb1_sb", [128, 32], F32)
    eb1 = sb("eb1_sb", [128, 32], F32)
    eb2 = sb("eb2_sb", [128, 8], F32)
    rb2 = sb("rb2_sb", [8, 1], F32)
    rw2 = sb("rw2_sb", [128, 32 * NE], BF16)

    AXA = sb("axa", [128, 16 * E], F32)
    xt_t = lambda t: AXA[:, E * t:E * (t + 1)]
    ew1 = AXA.bitcast(BF16)
    WBA = sb("wba", [128, 8 * HID], BF16)
    HB = sb("hb", [128, 32 * RG], BF16)
    HBf = HB.bitcast(F32)
    FMA = sb("fma", [128, 2 * 8 * RG], BF16)
    LFA = sb("lfa", [128, 8 * 512], F32)
    LFB = LFA.bitcast(BF16)
    PB = sb("pb", [128, 2 * E], F32)
    scores = PB[0:16, 0:2048]
    probsb = HBf[0:16, 0:2048]
    x1_t = lambda s: PB[:, E * s:E * (s + 1)]
    ln1c = sb("ln1c", [128, 2 * 128], F32)
    tmp1 = sb("tmp1", [128, E], F32)
    sq = sb("sq", [128, E], F32)
    ln2t = sb("ln2t", [128, 2 * E], BF16)
    stats = sb("stats", [128, 32], F32)
    probs_tm = sb("probs_tm", [128, 16 * 16], F32)
    vsr2 = sb("vsr2", [1, E], F32)
    xsum = sb("xsum", [128, 32], F32)
    xsT = sb("xsT", [32, 128], F32)
    xc = sb("xc", [128, 8], F32)
    veh = sb("veh", [128, 8], F32)
    te_b = sb("te_b", [8, 128], F32)
    den = HBf[0:16, 2048:3072]
    ltT = sb("ltT", [128, 128], F32)
    a1 = sb("a1", [128, 128], F32)
    a2 = sb("a2", [128, 128], F32)
    bsum = sb("bsum", [128, 128], F32)
    pfull = sb("pfull", [128, 128], F32)
    m1s = sb("m1s", [128, 16], F32)
    m2s = sb("m2s", [128, 16], F32)
    i1s = sb("i1s", [128, 16], F32)
    i2s = sb("i2s", [128, 16], F32)
    p1s = sb("p1s", [128, 16], F32)
    p2s = sb("p2s", [128, 16], F32)
    w1s = sb("w1s", [128, 16], F32)
    w2s = sb("w2s", [128, 16], F32)
    sc1 = sb("sc1", [128, 16], F32)
    sc2 = sb("sc2", [128, 16], F32)
    of1 = sb("of1", [128, 16], U32)
    of2 = sb("of2", [128, 16], U32)
    du1 = sb("du1", [128, 16], U32)
    du2 = sb("du2", [128, 16], U32)
    colsum = sb("colsum", [16, 8], F32)
    cnt_sb = sb("cnt_sb", [4, 8], F32)
    cag = sb("cag", [32, 8], F32)
    base = sb("base", [4, 8], F32)
    rall = sb("rall", [16, 8], F32)
    rrow = sb("rrow", [1, 128], F32)
    lgsb = sb("lgsb", [8, RG], F32)
    prod = FMA.bitcast(F32)[:, 0:1024]
    tblT = tmp1[:, 640:672]
    tmsk = tmp1[:, 672:704]
    idxg = tmp1[:, 704:736]
    idxu = sb("idxu", [128, 32], U32)
    tblr = ln1c[0:32, 0:128]

    ps = [ES.enter_context(nc.psum_tensor(f"ps{i}", [128, 512], F32)) for i in range(8)]

    sem_names = ["dS", "dG", "dA", "tPE", "tAC", "tVE", "cXS", "cCN", "cL2", "cTB", "cEO"]
    SEM = {n: ES.enter_context(nc.semaphore(n)) for n in sem_names}

    EV = []
    CNT = defaultdict(int)
    REGS = {}

    def op(eng, emit, waits=(), inc=None, amt=1):
        tok = None
        if inc is not None:
            CNT[inc] += amt
            tok = (inc, CNT[inc])
        EV.append((eng, tuple(w for w in waits if w is not None), emit, (inc, amt) if inc else None))
        return tok

    def dma(eng_code, out, in_, waits=(), sem="dS", **kw):
        return op(eng_code, lambda e: e.dma_start(out=out, in_=in_, **kw), waits, inc=sem, amt=16)

    tk_cp = dma("gp", cpk[:, :], cp_d[:, 0:CPW_SB], sem="dG")
    tk_zq = dma("sy", wq[:, :], cp_d[:, CO_ZERO:CO_ZERO + 64])
    tk_zk = dma("sy", wk[:, :], cp_d[:, CO_ZERO:CO_ZERO + 64])
    tk_wq = dma("sy", wq[0:64, :], wq_d, waits=[tk_zq])
    tk_wk = dma("sy", wk[0:64, :], wk_d, waits=[tk_zk])
    tk_wv = dma("gp", wv[:, :], wv_d, sem="dG")
    tk_rb1 = dma("gp", rb1[:, :], rb1_d, sem="dG")
    tk_eb1 = dma("gp", eb1[:, :], eb1_d, sem="dG")
    tk_eb2 = dma("gp", eb2[:, :], eb2_d, sem="dG")
    tk_rb2 = dma("gp", rb2[:, :], rb2_d, sem="dG")
    tk_rw2 = dma("gp", rw2[:, :], rw2_d, sem="dG")

    tk_m = op("te", lambda e: e.matmul(ps[6][0:64, 0:64], wq[:, :], wk[:, :], start=True, stop=True, skip_group_check=True),
              waits=[tk_wq, tk_wk], inc="tPE")
    tk_me = op("ac", lambda e: e.copy(m_sb[:, :], ps[6][0:64, 0:64]), waits=[tk_m], inc="tAC")
    tk_mz = dma("sy", mblk[:, :], cp_d[:, CO_ZERO:CO_ZERO + 128])
    tk_mb1 = op("ac", lambda e: e.copy(mblk[0:64, 0:64], m_sb[:, :]), waits=[tk_me, tk_mz], inc="tAC")
    tk_mst = dma("sy", mscr, m_sb[:, :], waits=[tk_me])
    tk_mb2 = dma("sy", mblk[64:128, 64:128], mscr, waits=[tk_mst, tk_mz])
    tk_wvt = op("te", lambda e: e.transpose(ps[7][0:64, 0:64], wv[:, :], IDENT[0:64, 0:64]),
                waits=[tk_wv, tk_cp, tk_m], inc="tPE")
    tk_wvte1 = op("ac", lambda e: e.copy(wvT[:, :], ps[7][0:64, 0:64]), waits=[tk_wvt], inc="tAC")
    tk_zv = dma("sy", wvblk[:, :], cp_d[:, CO_ZERO:CO_ZERO + 128])
    tk_wb1 = op("ac", lambda e: e.copy(wvblk[0:64, 0:64], wvT[:, :]), waits=[tk_wvte1, tk_zv], inc="tAC")
    tk_wvst = dma("sy", mscr, wvT[:, :], waits=[tk_wvte1, tk_mb2])
    tk_wvld = dma("sy", wvblk[64:128, 64:128], mscr, waits=[tk_wvst, tk_zv])

    tk_xld = [None] * NTILE
    scores_evac = [None] * 4
    fm_last = {}
    tr_war = {}
    ln1war = {}
    gwar = {}
    prodwar = {}
    sqwar = None
    for g in range(4):
        e16_first_wait = scores_evac[g - 1] if g > 0 else None
        for tl in range(4):
            tau = 4 * g + tl
            tk_xld[tau] = dma("sy" if tau % 2 == 0 else "gp", xt_t(tau),
                              x_my[128 * tau:128 * (tau + 1), :],
                              sem="dS" if tau % 2 == 0 else "dG")
            st = stats[:, 8 * (tau % 4):8 * (tau % 4) + 8]
            t_sq = op("ac", lambda e, tau=tau: e.activation(sq[:, :], xt_t(tau), AF.Square),
                      waits=[tk_xld[tau], sqwar], inc="tAC")
            t_s1 = op("ve", lambda e, st=st, tau=tau: e.reduce_sum(st[:, 0:1], xt_t(tau), axis=AX.X),
                      waits=[tk_xld[tau]], inc="tVE")
            t_s2 = op("ve", lambda e, st=st: e.reduce_sum(st[:, 1:2], sq[:, :], axis=AX.X),
                      waits=[t_sq], inc="tVE")
            sqwar = t_s2
            t_mn = op("ve", lambda e, st=st: e.tensor_scalar_mul(st[:, 2:3], st[:, 0:1], 1.0 / E),
                      waits=[t_s1], inc="tVE")
            t_ms = op("ve", lambda e, st=st: e.tensor_mul(st[:, 4:5], st[:, 2:3], st[:, 2:3]),
                      waits=[t_mn], inc="tVE")
            t_vr = op("ve", lambda e, st=st: e.scalar_tensor_tensor(
                st[:, 5:6], st[:, 1:2], 1.0 / E, st[:, 4:5], ALU.mult, ALU.subtract),
                waits=[t_ms], inc="tVE")
            t_sd = op("ac", lambda e, st=st: e.activation(st[:, 6:7], st[:, 5:6], AF.Sqrt, bias=EPSC[0:128, :]),
                      waits=[t_vr], inc="tAC")
            t_rs = op("ve", lambda e, st=st: e.reciprocal(st[:, 3:4], st[:, 6:7]),
                      waits=[t_sd], inc="tVE")
            for c in range(8):
                slot = c % 2
                t_ap = op("ve", lambda e, tau=tau, c=c, slot=slot, st=st: e.tensor_scalar(
                    ln1c[:, 128 * slot:128 * (slot + 1)], xt_t(tau)[:, 128 * c:128 * (c + 1)],
                    st[:, 2:3], st[:, 3:4], ALU.subtract, ALU.mult),
                    waits=[t_rs, ln1war.get(slot)], inc="tVE")
                par = c % 2
                t_tr = op("te", lambda e, par=par, slot=slot: e.transpose(
                    ps[par][:, 0:128], ln1c[:, 128 * slot:128 * (slot + 1)], IDENT),
                    waits=[t_ap, tr_war.get(par)], inc="tPE")
                ln1war[slot] = t_tr
                t_ev = op("ac", lambda e, c=c, tl=tl, par=par: e.copy(
                    LFA[:, 512 * c + 128 * tl:512 * c + 128 * (tl + 1)], ps[par][:, 0:128]),
                    waits=[t_tr], inc="tAC")
                tr_war[par] = t_ev
                if tl == 3:
                    fm_last[(g, c)] = t_ev
        for c in range(8):
            par = 2 + (c % 2)
            t_g = op("te", lambda e, c=c, par=par: e.matmul(
                ps[par][:, 0:512], mblk[:, :], LFA[:, 512 * c:512 * (c + 1)], start=True, stop=True, skip_group_check=True),
                waits=[fm_last[(g, c)], tk_mb1, tk_mb2, gwar.get(par)], inc="tPE")
            t_p = op("ve", lambda e, c=c, par=par: e.tensor_mul(
                prod[:, 512 * (c % 2):512 * (c % 2 + 1)], LFA[:, 512 * c:512 * (c + 1)], ps[par][:, 0:512]),
                waits=[t_g, prodwar.get(c % 2)], inc="tVE")
            gwar[par] = t_p
            t_e16 = op("te", lambda e, c=c: e.matmul(
                ps[4][0:16, 0:512], E16[:, 16 * c:16 * (c + 1)], prod[:, 512 * (c % 2):512 * (c % 2 + 1)],
                start=(c == 0), stop=(c == 7), skip_group_check=True),
                waits=[t_p] + ([e16_first_wait] if c == 0 else []), inc="tPE")
            prodwar[c % 2] = t_e16
            op("ve", lambda e, g=g, c=c: e.reduce_sum(
                xsum[:, 8 * g + c:8 * g + c + 1], LFA[:, 512 * c:512 * (c + 1)], axis=AX.X),
                waits=[fm_last[(g, c)]], inc="tVE")
        scores_evac[g] = op("ac", lambda e, g=g: e.activation(
            scores[:, 512 * g:512 * (g + 1)], ps[4][0:16, 0:512], AF.Copy, scale=I32C[0:16, :]),
            waits=[prodwar[1]], inc="tAC")
    tk_xsum_all = ("tVE", CNT["tVE"])
    tk_rw1k = [dma("gp", WBA[:, HID * k:HID * (k + 1)].rearrange("p (s q) -> p s q", q=2048),
                   rw1_d[:, HID * k:HID * (k + 1)].rearrange("p (s q) -> p s q", q=2048), sem="dG")
               for k in range(8)]

    t_xt = op("te", lambda e: e.transpose(ps[6][0:32, 0:128], xsum[:, :], IDENT),
              waits=[tk_xsum_all, tk_me], inc="tPE")
    t_xte = op("ac", lambda e: e.copy(xsT[:, :], ps[6][0:32, 0:128]), waits=[t_xt], inc="tAC")
    t_xst = dma("sy", xsb.rearrange("a (k p) -> (a k) p", p=128), xsT[:, :], waits=[t_xte])
    tk_ccxs = op("gp", lambda e: e.collective_compute(
        "AllReduce", ALU.add, replica_groups=[list(range(NCORE))], ins=[xsb], outs=[xsr]),
        waits=[t_xst], inc="cXS")
    t_exp = op("ac", lambda e: e.activation(probsb[:, :], scores[:, :], AF.Exp),
               waits=scores_evac, inc="tAC")
    t_d1 = op("ve", lambda e: e.tensor_add(den[:, 0:512], probsb[:, 0:512], probsb[:, 512:1024]),
              waits=[t_exp], inc="tVE")
    t_d2 = op("ve", lambda e: e.tensor_add(tmp1[0:16, 0:512], probsb[:, 1024:1536], probsb[:, 1536:2048]),
              waits=[t_exp], inc="tVE")
    t_d3 = op("ve", lambda e: e.tensor_add(den[:, 512:1024], den[:, 0:512], tmp1[0:16, 0:512]),
              waits=[t_d1, t_d2], inc="tVE")
    t_rc = op("ve", lambda e: e.reciprocal(tmp1[0:16, 512:1024], den[:, 512:1024]),
              waits=[t_d3], inc="tVE")
    tk_dsc = None
    if dbg:
        tk_dsc = op("gp", lambda e: e.dma_start(out=d_sc, in_=scores[:, :]),
                    waits=scores_evac, inc="dG", amt=16)
    pt_toks = []
    for cch in range(4):
        pt_toks.append(op("ve", lambda e, cch=cch: e.tensor_mul(
            scores[:, 512 * cch:512 * (cch + 1)], probsb[:, 512 * cch:512 * (cch + 1)],
            tmp1[0:16, 512:1024]),
            waits=[t_rc, tk_dsc], inc="tVE"))
    probs_tm_ev = {}
    for cch in range(4):
        for tl in range(4):
            tau = 4 * cch + tl
            par = tau % 2
            t_tr = op("te", lambda e, cch=cch, tl=tl, par=par: e.transpose(
                ps[par][:, 0:16], scores[:, 512 * cch + 128 * tl:512 * cch + 128 * (tl + 1)],
                IDENT[0:16, 0:16]),
                waits=[pt_toks[cch], tr_war.get(par)], inc="tPE")
            t_ev = op("ac", lambda e, tau=tau, par=par: e.copy(
                probs_tm[:, 16 * tau:16 * (tau + 1)], ps[par][:, 0:16]),
                waits=[t_tr], inc="tAC")
            tr_war[par] = t_ev
            probs_tm_ev[tau] = t_ev
    tk_probs_done = probs_tm_ev[NTILE - 1]
    vsr_ld = [None] * 4
    ps5war = None
    teb_war = (None, None)
    for cch in range(4):
        t_l = dma("sy", tblr[0:8, :], xsr[cch, :].rearrange("(k p) -> k p", p=128), waits=[tk_ccxs, ps5war])
        t_t = op("te", lambda e: e.transpose(ps[5][:, 0:8], tblr[0:8, :], IDENT[0:8, 0:8]),
                 waits=[t_l, ps5war], inc="tPE")
        t_e = op("ac", lambda e: e.copy(xc[:, :], ps[5][:, 0:8]), waits=[t_t], inc="tAC")
        t_v = op("te", lambda e: e.matmul(ps[5][:, 8:16], wvblk[:, :], xc[:, :],
                                          start=True, stop=True, skip_group_check=True),
                 waits=[t_e, tk_wb1, tk_wvld], inc="tPE")
        t_ve = op("ac", lambda e: e.copy(veh[:, :], ps[5][:, 8:16]), waits=[t_v], inc="tAC")
        t_tb = op("te", lambda e: e.transpose(ps[5][0:8, 16:144], veh[:, :], IDENT),
                  waits=[t_ve], inc="tPE")
        t_fb = op("ac", lambda e: e.copy(te_b[:, :], ps[5][0:8, 16:144]),
                  waits=[t_tb, teb_war[0], teb_war[1]], inc="tAC")
        ps5war = t_fb
        t_s1 = dma("sy", vscr[cch, :].rearrange("(k two e) -> k two e", two=2, e=64)[:, 0:1, :],
                   te_b[:, 0:64].unsqueeze(1), waits=[t_fb])
        t_s2 = dma("sy", vscr[cch, :].rearrange("(k two e) -> k two e", two=2, e=64)[:, 1:2, :],
                   te_b[:, 64:128].unsqueeze(1), waits=[t_fb])
        teb_war = (t_s1, t_s2)
        vsr_ld[cch] = (t_s1, t_s2)

    x1st = [None] * NTILE
    ln2st = [None] * NTILE
    vbwar = {2: gwar.get(2), 3: gwar.get(3), 6: t_xte, 7: tk_wvte1}
    x1war = {}
    ln2war = {}
    vbst = {}
    vb_pe = None
    tmplast = None
    lg_ev = {}
    fmwar = {}
    t_lds_of = {}
    hwar = {0: tr_war.get(0), 1: tr_war.get(1)}
    pexwar = {4: scores_evac[3], 5: ps5war}

    def emit_vb(g):
        nonlocal vb_pe
        b0, b1 = (2, 3) if g % 2 == 0 else (6, 7)
        t_vld = dma("sy", vsr2[0:1, :], vscr[g, :].rearrange("(a b) -> a b", a=1),
                    waits=[vsr_ld[g][0], vsr_ld[g][1], vb_pe])
        t_vb0 = op("te", lambda e, b0=b0: e.matmul(ps[b0][:, 0:512], ONES1, vsr2[0:1, 0:512],
                                                   start=True, stop=True, skip_group_check=True),
                   waits=[t_vld, vbwar.get(b0)], inc="tPE")
        t_vb1 = op("te", lambda e, b1=b1: e.matmul(ps[b1][:, 0:512], ONES1, vsr2[0:1, 512:1024],
                                                   start=True, stop=True, skip_group_check=True),
                   waits=[t_vld, vbwar.get(b1)], inc="tPE")
        vb_pe = t_vb1
        vbst[g] = (t_vb0, t_vb1, b0, b1)

    def emit_p1b_tiles(g):
        nonlocal sqwar, tmplast
        t_vb0, t_vb1, b0, b1 = vbst[g]
        for tl in range(4):
            tau = 4 * g + tl
            pex = lambda tau, h0: probs_tm[:, 16 * tau + h0:16 * tau + h0 + 8].unsqueeze(2).broadcast_to([128, 8, 64])
            t_t0 = op("ve", lambda e, tau=tau, b0=b0: e.tensor_mul(
                tmp1[:, 0:512].rearrange("p (h d) -> p h d", d=64), pex(tau, 0),
                ps[b0][:, 0:512].rearrange("p (h d) -> p h d", d=64)),
                waits=[t_vb0, probs_tm_ev[tau]], inc="tVE")
            t_t1 = op("ve", lambda e, tau=tau, b1=b1: e.tensor_mul(
                tmp1[:, 512:1024].rearrange("p (h d) -> p h d", d=64), pex(tau, 8),
                ps[b1][:, 0:512].rearrange("p (h d) -> p h d", d=64)),
                waits=[t_vb1, probs_tm_ev[tau]], inc="tVE")
            if tl == 3:
                vbwar[b0] = t_t0
                vbwar[b1] = t_t1
            xs = tau % 2
            t_x1 = op("ve", lambda e, tau=tau, xs=xs: e.tensor_add(x1_t(xs), xt_t(tau), tmp1[:, :]),
                      waits=[t_t0, t_t1, tk_probs_done] + list(x1war.get(xs, ())), inc="tVE")
            tmplast = t_x1
            x1st[tau] = dma("sy", x1my[128 * tau:128 * (tau + 1), :], x1_t(xs), waits=[t_x1])
            st = stats[:, 8 * (tau % 4):8 * (tau % 4) + 8]
            t_sq = op("ac", lambda e, xs=xs: e.activation(sq[:, :], x1_t(xs), AF.Square),
                      waits=[t_x1, sqwar], inc="tAC")
            t_s1 = op("ve", lambda e, st=st, xs=xs: e.reduce_sum(st[:, 0:1], x1_t(xs), axis=AX.X),
                      waits=[t_x1], inc="tVE")
            t_s2 = op("ve", lambda e, st=st: e.reduce_sum(st[:, 1:2], sq[:, :], axis=AX.X),
                      waits=[t_sq], inc="tVE")
            sqwar = t_s2
            t_mn = op("ve", lambda e, st=st: e.tensor_scalar_mul(st[:, 2:3], st[:, 0:1], 1.0 / E),
                      waits=[t_s1], inc="tVE")
            t_ms = op("ve", lambda e, st=st: e.tensor_mul(st[:, 4:5], st[:, 2:3], st[:, 2:3]),
                      waits=[t_mn], inc="tVE")
            t_vr = op("ve", lambda e, st=st: e.scalar_tensor_tensor(
                st[:, 5:6], st[:, 1:2], 1.0 / E, st[:, 4:5], ALU.mult, ALU.subtract),
                waits=[t_ms], inc="tVE")
            t_sd = op("ac", lambda e, st=st: e.activation(st[:, 6:7], st[:, 5:6], AF.Sqrt, bias=EPSC[0:128, :]),
                      waits=[t_vr], inc="tAC")
            t_rs = op("ve", lambda e, st=st: e.reciprocal(st[:, 3:4], st[:, 6:7]),
                      waits=[t_sd], inc="tVE")
            slot = tau % 2
            t_ln2 = op("ve", lambda e, st=st, xs=xs, slot=slot: e.tensor_scalar(
                ln2t[:, E * slot:E * (slot + 1)], x1_t(xs), st[:, 2:3], st[:, 3:4],
                ALU.subtract, ALU.mult),
                waits=[t_rs, ln2war.get(slot)], inc="tVE")
            ln2st[tau] = dma("sy", ln2my[128 * tau:128 * (tau + 1), :], ln2t[:, E * slot:E * (slot + 1)],
                             waits=[t_ln2])
            ln2war[slot] = ln2st[tau]
            x1war[xs] = (t_ln2, x1st[tau])

    def emit_p2_chain(gr):
        buf = gr % 2
        fmo = 8 * RG * buf
        t_lds = []
        for cch in range(8):
            t_lds.append(dma("sy", FMA[:, fmo + RG * cch:fmo + RG * (cch + 1)],
                             ln2my[RG * gr:RG * (gr + 1), 128 * cch:128 * (cch + 1)],
                             waits=[ln2st[2 * gr], ln2st[2 * gr + 1], fmwar.get(buf)],
                             transpose=True))
        t_lds_of[gr] = t_lds

    def emit_p2_compute(gr):
        buf = gr % 2
        fmo = 8 * RG * buf
        t_lds = t_lds_of[gr]
        hev = {}
        for ht in range(32):
            par = ht % 2
            tkm = None
            for k in range(8):
                w = [t_lds[k]]
                if ht == 0:
                    w += [tk_rw1k[k]]
                if k == 0:
                    w += [hwar.get(par)]
                tkm = op("te", lambda e, ht=ht, k=k, par=par, fmo=fmo: e.matmul(
                    ps[par][:, 0:RG], WBA[:, HID * k + 128 * ht:HID * k + 128 * (ht + 1)],
                    FMA[:, fmo + RG * k:fmo + RG * (k + 1)],
                    start=(k == 0), stop=(k == 7), skip_group_check=True),
                    waits=w, inc="tPE" if k == 7 else None)
            t_ev = op("ac", lambda e, ht=ht, par=par: e.activation(
                HB[:, RG * ht:RG * (ht + 1)], ps[par][:, 0:RG], AF.Relu, bias=rb1[:, ht:ht + 1]),
                waits=[tkm, tk_rb1], inc="tAC")
            hwar[par] = t_ev
            hev[ht] = t_ev
        fmwar[buf] = hev[31]
        t_lg = None
        for k2 in range(32):
            w = [hev[k2]]
            if k2 == 0:
                w += [tk_rw2, pexwar.get(4)]
            t_lg = op("te", lambda e, k2=k2: e.matmul(
                ps[4][0:8, 0:RG], rw2[:, 8 * k2:8 * (k2 + 1)], HB[:, RG * k2:RG * (k2 + 1)],
                start=(k2 == 0), stop=(k2 == 31), skip_group_check=True),
                waits=w, inc="tPE" if k2 == 31 else None)
        t_lge = op("ac", lambda e: e.activation(lgsb[:, :], ps[4][0:8, 0:RG], AF.Identity, bias=rb2[:, 0:1]),
                   waits=[t_lg, tk_rb2], inc="tAC")
        pexwar[4] = t_lge
        for tl in range(2):
            tau = 2 * gr + tl
            t_tr = op("te", lambda e, tl=tl: e.transpose(ps[5][:, 0:8], lgsb[:, 128 * tl:128 * (tl + 1)],
                                                         IDENT[0:8, 0:8]),
                      waits=[t_lge, pexwar.get(5)], inc="tPE")
            t_ev = op("ac", lambda e, tau=tau: e.copy(ltT[:, 8 * tau:8 * (tau + 1)], ps[5][:, 0:8]),
                      waits=[t_tr], inc="tAC")
            pexwar[5] = t_ev
            lg_ev[tau] = t_ev

    emit_vb(0)
    emit_vb(1)
    for g in range(4):
        emit_p1b_tiles(g)
        if g >= 1:
            emit_p2_compute(2 * g - 2)
            emit_p2_compute(2 * g - 1)
        if g < 2:
            emit_vb(g + 2)
        emit_p2_chain(2 * g)
        emit_p2_chain(2 * g + 1)
        lastx1 = x1war[(4 * g + 3) % 2][0]
        for half in range(2):
            k = 2 * g + half
            dma("gp", ew1[:, 4096 * k:4096 * (k + 1)].rearrange("p (s q) -> p s q", q=2048),
                ew1_d[:, 4096 * k:4096 * (k + 1)].rearrange("p (s q) -> p s q", q=2048),
                waits=[lastx1], sem="dG")
    tk_ew1 = ("dG", CNT["dG"])
    tk_ccl2 = op("gp", lambda e: e.collective_compute(
        "AllGather", ALU.bypass, replica_groups=[list(range(NCORE))], ins=[ln2my], outs=[ln2all]),
        waits=ln2st, inc="cL2")
    emit_p2_compute(6)
    emit_p2_compute(7)
    tk_router_pe = ("tPE", CNT["tPE"])
    for k in range(8):
        dma("gp", WBA[:, 4096 * k:4096 * (k + 1)].rearrange("p (s q) -> p s q", q=2048),
            ew2_d[:, 4096 * k:4096 * (k + 1)].rearrange("p (s q) -> p s q", q=2048),
            waits=[tk_router_pe], sem="dG")
    tk_ew2 = ("dG", CNT["dG"])

    LGV = ltT[:, :].rearrange("p (t e) -> p t e", e=8)
    lgall = list(lg_ev.values())
    t_m1 = op("ve", lambda e: e.reduce_max(m1s[:, :], LGV, axis=AX.X), waits=lgall, inc="tVE")
    m1b = m1s[:, :].unsqueeze(2).broadcast_to([128, 16, 8])
    t_eq = op("ve", lambda e: e.tensor_tensor(a1[:, :].rearrange("p (t e) -> p t e", e=8), LGV, m1b, ALU.is_equal),
              waits=[t_m1], inc="tVE")
    t_tm = op("ve", lambda e: e.tensor_mul(a2[:, :], a1[:, :], REV8), waits=[t_eq], inc="tVE")
    t_r1 = op("ve", lambda e: e.reduce_max(w2s[:, :], a2[:, :].rearrange("p (t e) -> p t e", e=8), axis=AX.X),
              waits=[t_tm], inc="tVE")
    t_i1 = op("ve", lambda e: e.tensor_scalar(i1s[:, :], w2s[:, :], -1.0, 8.0, ALU.mult, ALU.add),
              waits=[t_r1], inc="tVE")
    i1b = i1s[:, :].unsqueeze(2).broadcast_to([128, 16, 8])
    t_a1 = op("ve", lambda e: e.tensor_tensor(a1[:, :].rearrange("p (t e) -> p t e", e=8),
                                              IOTA8.rearrange("p (t e) -> p t e", e=8), i1b, ALU.is_equal),
              waits=[t_i1], inc="tVE")
    t_l2 = op("ve", lambda e: e.scalar_tensor_tensor(bsum[:, :], a1[:, :], -1e9, ltT[:, :], ALU.mult, ALU.add),
              waits=[t_a1], inc="tVE")
    BSV = bsum[:, :].rearrange("p (t e) -> p t e", e=8)
    t_m2 = op("ve", lambda e: e.reduce_max(m2s[:, :], BSV, axis=AX.X), waits=[t_l2], inc="tVE")
    m2b = m2s[:, :].unsqueeze(2).broadcast_to([128, 16, 8])
    t_eq2 = op("ve", lambda e: e.tensor_tensor(a2[:, :].rearrange("p (t e) -> p t e", e=8), BSV, m2b, ALU.is_equal),
               waits=[t_m2], inc="tVE")
    t_tm2 = op("ve", lambda e: e.tensor_mul(bsum[:, :], a2[:, :], REV8), waits=[t_eq2], inc="tVE")
    t_r2 = op("ve", lambda e: e.reduce_max(w2s[:, :], bsum[:, :].rearrange("p (t e) -> p t e", e=8), axis=AX.X),
              waits=[t_tm2], inc="tVE")
    t_i2 = op("ve", lambda e: e.tensor_scalar(i2s[:, :], w2s[:, :], -1.0, 8.0, ALU.mult, ALU.add),
              waits=[t_r2], inc="tVE")
    i2b = i2s[:, :].unsqueeze(2).broadcast_to([128, 16, 8])
    t_a2 = op("ve", lambda e: e.tensor_tensor(a2[:, :].rearrange("p (t e) -> p t e", e=8),
                                              IOTA8.rearrange("p (t e) -> p t e", e=8), i2b, ALU.is_equal),
              waits=[t_i2], inc="tVE")
    t_df = op("ve", lambda e: e.tensor_sub(w1s[:, :], m2s[:, :], m1s[:, :]), waits=[t_m2], inc="tVE")
    t_sg = op("ac", lambda e: e.activation(p2s[:, :], w1s[:, :], AF.Sigmoid), waits=[t_df], inc="tAC")
    t_p1 = op("ve", lambda e: e.tensor_scalar(p1s[:, :], p2s[:, :], -1.0, 1.0, ALU.mult, ALU.add),
              waits=[t_sg], inc="tVE")
    t_b = op("ve", lambda e: e.tensor_add(bsum[:, :], a1[:, :], a2[:, :]), waits=[t_a1, t_a2], inc="tVE")

    t_pe_cols = []
    for t in range(16):
        tk = op("te", lambda e, t=t: e.matmul(ps[4][:, 8 * t:8 * (t + 1)], L128,
                                              bsum[:, 8 * t:8 * (t + 1)], start=(t == 0), stop=True, skip_group_check=True),
                waits=[t_b, pexwar.get(4)], inc="tPE")
        t_pe_cols.append(tk)
        op("te", lambda e, t=t: e.matmul(ps[5][0:16, 0:8], UCOLS[:, 16 * t:16 * (t + 1)],
                                         bsum[:, 8 * t:8 * (t + 1)], start=(t == 0), stop=(t == 15), skip_group_check=True),
           waits=[t_b, pexwar.get(5)], inc="tPE")
    tk_cs = ("tPE", CNT["tPE"])
    t_cse = op("ac", lambda e: e.copy(colsum[:, :], ps[5][0:16, 0:8]), waits=[tk_cs], inc="tAC")
    t_cnt = op("te", lambda e: e.matmul(ps[5][0:4, 8:16], C4, colsum[:, :], start=True, stop=True, skip_group_check=True),
               waits=[t_cse], inc="tPE")
    t_cnte = op("ac", lambda e: e.copy(cnt_sb[:, :], ps[5][0:4, 8:16]), waits=[t_cnt], inc="tAC")
    t_cst = dma("sy", cntb, cnt_sb[:, :], waits=[t_cnte])
    t_cc = op("gp", lambda e: e.collective_compute(
        "AllGather", ALU.bypass, replica_groups=[list(range(NCORE))], ins=[cntb], outs=[cntg]),
        waits=[t_cst], inc="cCN")
    t_cgl = dma("sy", cag[:, :], cntg, waits=[t_cc])
    t_bs = op("te", lambda e: e.matmul(ps[5][0:4, 16:24], WJ, cag[:, :], start=True, stop=True, skip_group_check=True),
              waits=[t_cgl], inc="tPE")
    t_bse = op("ac", lambda e: e.copy(base[:, :], ps[5][0:4, 16:24]), waits=[t_bs], inc="tAC")
    t_r1m = op("te", lambda e: e.matmul(ps[5][0:16, 24:32], E4, base[:, :], start=True, stop=False, skip_group_check=True),
               waits=[t_bse], inc="tPE")
    t_r2m = op("te", lambda e: e.matmul(ps[5][0:16, 24:32], LBLK, colsum[:, :], start=False, stop=True, skip_group_check=True),
               waits=[t_cse], inc="tPE")
    t_re = op("ac", lambda e: e.copy(rall[:, :], ps[5][0:16, 24:32]), waits=[t_r2m], inc="tAC")
    t_rst = dma("sy", rscr, rall[:, :], waits=[t_re])
    t_rld = dma("sy", rrow[0:1, :], rscr.rearrange("a b -> (a b)").rearrange("(x f) -> x f", x=1),
                waits=[t_rst])
    t_k1 = []
    for t in range(16):
        t_k1.append(op("te", lambda e, t=t: e.matmul(ps[4][:, 8 * t:8 * (t + 1)], ONES1,
                                                     rrow[0:1, 8 * t:8 * (t + 1)], start=False, stop=True, skip_group_check=True),
                       waits=[t_rld, t_pe_cols[t]], inc="tPE"))
    t_pf = op("ve", lambda e: e.tensor_copy(pfull[:, :], ps[4][:, 0:128]), waits=t_k1, inc="tVE")
    t_q1 = op("ve", lambda e: e.tensor_mul(bsum[:, :], a1[:, :], pfull[:, :]), waits=[t_pf], inc="tVE")
    t_po1 = op("ve", lambda e: e.reduce_sum(w1s[:, :], bsum[:, :].rearrange("p (t e) -> p t e", e=8), axis=AX.X),
               waits=[t_q1], inc="tVE")
    t_q2 = op("ve", lambda e: e.tensor_mul(bsum[:, :], a2[:, :], pfull[:, :]), waits=[t_po1], inc="tVE")
    t_po2 = op("ve", lambda e: e.reduce_sum(w2s[:, :], bsum[:, :].rearrange("p (t e) -> p t e", e=8), axis=AX.X),
               waits=[t_q2], inc="tVE")
    sc_toks = []
    du_toks = []
    of_toks = []
    for kk, (poss, idxs, prb, scs, ofs, dus, tpo) in enumerate((
            (w1s, i1s, p1s, sc1, of1, du1, t_po1), (w2s, i2s, p2s, sc2, of2, du2, t_po2))):
        kb = 512 * kk
        t_kp = op("ve", lambda e, poss=poss, kb=kb: e.tensor_scalar(
            tmp1[0:128, kb + 0:kb + 16], poss[:, :], float(CAP), None, ALU.is_lt), waits=[tpo], inc="tVE")
        t_sc = op("ve", lambda e, prb=prb, scs=scs, kb=kb: e.tensor_mul(
            scs[:, :], prb[:, :], tmp1[0:128, kb + 0:kb + 16]),
            waits=[t_kp, t_p1], inc="tVE")
        sc_toks.append(t_sc)
        t_pc = op("ve", lambda e, poss=poss, kb=kb: e.tensor_scalar_min(
            tmp1[0:128, kb + 16:kb + 32], poss[:, :], float(CAP - 1)),
            waits=[tpo], inc="tVE")
        t_gs = []
        for i in range(1, 8):
            t_gs.append(op("ve", lambda e, kb=kb, i=i: e.tensor_scalar(
                tmp1[0:128, kb + 112 + 16 * (i - 1):kb + 128 + 16 * (i - 1)],
                tmp1[0:128, kb + 16:kb + 32], 512.0 * i, None, ALU.is_ge),
                waits=[t_pc], inc="tVE"))
        acc = [kb + 224, kb + 240]
        t_acc = op("ve", lambda e, kb=kb: e.tensor_add(
            tmp1[0:128, kb + 224:kb + 240], tmp1[0:128, kb + 112:kb + 128],
            tmp1[0:128, kb + 128:kb + 144]),
            waits=[t_gs[0], t_gs[1]], inc="tVE")
        cur = 0
        for i in range(3, 8):
            src, dst = acc[cur], acc[1 - cur]
            t_acc = op("ve", lambda e, i=i, kb=kb, src=src, dst=dst: e.tensor_add(
                tmp1[0:128, dst:dst + 16], tmp1[0:128, src:src + 16],
                tmp1[0:128, kb + 112 + 16 * (i - 1):kb + 128 + 16 * (i - 1)]),
                waits=[t_acc, t_gs[i - 1]], inc="tVE")
            cur = 1 - cur
        accf = acc[cur]
        t_of0 = op("ve", lambda e, idxs=idxs, kb=kb: e.scalar_tensor_tensor(
            tmp1[0:128, kb + 32:kb + 48], idxs[:, :], 512.0, tmp1[0:128, kb + 16:kb + 32],
            ALU.mult, ALU.add),
            waits=[t_pc], inc="tVE")
        t_of = op("ve", lambda e, kb=kb, accf=accf: e.scalar_tensor_tensor(
            tmp1[0:128, kb + 32:kb + 48], tmp1[0:128, accf:accf + 16], 3584.0,
            tmp1[0:128, kb + 32:kb + 48], ALU.mult, ALU.add),
            waits=[t_acc, t_of0], inc="tVE")
        t_ofu = op("ac", lambda e, ofs=ofs, kb=kb: e.copy(ofs[:, :], tmp1[0:128, kb + 32:kb + 48]),
                   waits=[t_of], inc="tAC")
        of_toks.append(t_ofu)
        t_d0 = op("ve", lambda e, idxs=idxs, poss=poss, kb=kb: e.scalar_tensor_tensor(
            tmp1[0:128, kb + 48:kb + 64], idxs[:, :], float(CAP), poss[:, :], ALU.mult, ALU.add),
            waits=[tpo], inc="tVE")
        t_b1 = op("ve", lambda e, kb=kb: e.tensor_sub(
            tmp1[0:128, kb + 64:kb + 80], tmp1[0:128, kb + 48:kb + 64], OOBC),
            waits=[t_d0], inc="tVE")
        t_b2 = op("ve", lambda e, kb=kb: e.tensor_mul(
            tmp1[0:128, kb + 80:kb + 96], tmp1[0:128, kb + 64:kb + 80], tmp1[0:128, kb + 0:kb + 16]),
            waits=[t_b1, t_kp], inc="tVE")
        t_b3 = op("ve", lambda e, kb=kb: e.tensor_add(
            tmp1[0:128, kb + 96:kb + 112], tmp1[0:128, kb + 80:kb + 96], OOBC),
            waits=[t_b2], inc="tVE")
        t_du = op("ac", lambda e, dus=dus, kb=kb: e.copy(dus[:, :], tmp1[0:128, kb + 96:kb + 112]),
                  waits=[t_b3], inc="tAC")
        du_toks.append(t_du)
    tk_z256 = op("ve", lambda e: e.memset(tmp1[:, 768:1024], 0.0), waits=[t_pf], inc="tVE")
    t_zt = dma("sy", tbl_d.rearrange("(p f) -> p f", p=128), tmp1[:, 768:1024], waits=[tk_z256])
    sc_dma = []
    for kk, dus in ((0, du1), (1, du2)):
        for t in range(16):
            sc_dma.append(op("gp", lambda e, dus=dus, t=t: e.indirect_dma_start(
                out=tbl_d.rearrange("(a b) -> a b", b=1),
                out_offset=IndirectOffsetOnAxis(dus[:, t:t + 1], axis=0),
                in_=VALB[:, t:t + 1], in_offset=None,
                bounds_check=REGS['b32k'], oob_is_err=False),
                waits=[du_toks[kk], t_zt, tk_cp], inc="dG", amt=16))
    t_rs_ = op("gp", lambda e: e.collective_compute(
        "ReduceScatter", ALU.add, replica_groups=[list(range(NCORE))],
        ins=[tbl_d.rearrange("(a b) -> a b", b=128)], outs=[tblrs]),
        waits=sc_dma, inc="cTB")

    t_tld = dma("sy", tblr[:, :], tblrs, waits=[t_rs_, ps5war])
    t_ttr = op("te", lambda e: e.transpose(ps[5][:, 0:32], tblr[:, :], IDENT[0:32, 0:32]),
               waits=[t_tld, t_re], inc="tPE")
    t_tte = op("ac", lambda e: e.copy(tblT[:, :], ps[5][:, 0:32]), waits=[t_ttr], inc="tAC")
    t_tm_ = op("ve", lambda e: e.tensor_scalar(tmsk[:, :], tblT[:, :], 0.5, None, ALU.is_lt),
               waits=[t_tte], inc="tVE")
    t_tg1 = op("ve", lambda e: e.scalar_tensor_tensor(idxg[:, :], tmsk[:, :], 1e6, tblT[:, :],
                                                      ALU.mult, ALU.add),
               waits=[t_tm_], inc="tVE")
    t_tg2 = op("ve", lambda e: e.tensor_scalar_add(tmsk[:, :], idxg[:, :], -1.0), waits=[t_tg1], inc="tVE")
    t_idxu = op("ac", lambda e: e.copy(idxu[:, :], tmsk[:, :]), waits=[t_tg2], inc="tAC")

    dtm = lambda r: LFB[:, 1024 * r:1024 * (r + 1)]
    etm = lambda r: LFB[:, 2048 + 1024 * r:2048 + 1024 * (r + 1)]
    efm = lambda c: LFB[:, 4096 + RG * c:4096 + RG * (c + 1)]
    dtm_war = {}
    etm_war = {}
    efm_war = {}
    dfm_war = {}
    hwar3 = {}
    eowar = {}
    dfm_of = {}
    eost_grp = {}
    cceo_n = 0

    def emit_chain(sg):
        buf = sg % 2
        fmo = 8 * RG * buf
        t_gas = {}
        for half in range(2):
            s0 = RG * sg + 128 * half
            r = half
            t_gas[r] = op("gp", lambda e, s0=s0, r=r: e.indirect_dma_start(
                out=dtm(r), out_offset=None, in_=ln2all,
                in_offset=IndirectOffsetOnAxis(idxu[:, s0 // 128:s0 // 128 + 1], axis=0),
                bounds_check=REGS['b16k'], oob_is_err=False),
                waits=[t_idxu, tk_ccl2, dtm_war.get(r)], inc="dG", amt=16)
        t_dfm = {}
        for cch in range(8):
            for r in range(2):
                t_dfm[(cch, r)] = dma(
                    "sy", FMA[:, fmo + RG * cch + 128 * r:fmo + RG * cch + 128 * (r + 1)],
                    dtm(r)[:, 128 * cch:128 * (cch + 1)],
                    waits=[t_gas[r], dfm_war.get(buf)], transpose=True)
        for r in range(2):
            dtm_war[r] = t_dfm[(7, r)]
        dfm_of[sg] = t_dfm

    def emit_compute(sg):
        buf = sg % 2
        fmo = 8 * RG * buf
        t_dfm = dfm_of[sg]
        hev = {}
        for ht in range(32):
            par = ht % 2
            tkm = None
            for k in range(8):
                w = [t_dfm[(k, 0)], t_dfm[(k, 1)]]
                if k == 0:
                    w += [tk_ew1, hwar3.get(par, hwar.get(par))]
                tkm = op("te", lambda e, ht=ht, k=k, par=par, fmo=fmo: e.matmul(
                    ps[par][:, 0:RG], ew1[:, 4096 * k + 128 * ht:4096 * k + 128 * (ht + 1)],
                    FMA[:, fmo + RG * k:fmo + RG * (k + 1)],
                    start=(k == 0), stop=(k == 7), skip_group_check=True),
                    waits=w, inc="tPE" if k == 7 else None)
            t_ev = op("ac", lambda e, ht=ht, par=par: e.activation(
                HB[:, RG * ht:RG * (ht + 1)], ps[par][:, 0:RG], AF.Relu, bias=eb1[:, ht:ht + 1]),
                waits=[tkm, tk_eb1], inc="tAC")
            hwar3[par] = t_ev
            hev[ht] = t_ev
        dfm_war[buf] = hev[31]
        for ot in range(8):
            par = 2 + ot % 2
            tkm = None
            for k2 in range(32):
                w = [hev[k2]]
                if k2 == 0:
                    w += [tk_ew2, eowar.get(par, vbwar.get(par))]
                tkm = op("te", lambda e, ot=ot, k2=k2, par=par: e.matmul(
                    ps[par][:, 0:RG], WBA[:, 1024 * k2 + 128 * ot:1024 * k2 + 128 * (ot + 1)],
                    HB[:, RG * k2:RG * (k2 + 1)],
                    start=(k2 == 0), stop=(k2 == 31), skip_group_check=True),
                    waits=w, inc="tPE" if k2 == 31 else None)
            t_ev = op("ac", lambda e, ot=ot, par=par: e.activation(
                efm(ot), ps[par][:, 0:RG], AF.Identity, bias=eb2[:, ot:ot + 1]),
                waits=[tkm, tk_eb2, efm_war.get(ot)], inc="tAC")
            eowar[par] = t_ev
            t_st = dma("sy", eoD[128 * ot:128 * (ot + 1), RG * sg:RG * (sg + 1)], efm(ot),
                       waits=[t_ev])
            efm_war[ot] = t_st
        grp_st = [efm_war[ot] for ot in range(8)]
        es = []
        for half in range(2):
            s0 = RG * sg + 128 * half
            r = half
            t_tl = dma("sy", etm(r), eoD[:, s0:s0 + 128], waits=grp_st + [etm_war.get(r)],
                       transpose=True)
            t_es = dma("sy", eomy[s0:s0 + 128, :], etm(r), waits=[t_tl])
            etm_war[r] = t_es
            es.append(t_es)
        eost_grp[sg] = es

    for sg in range(NSG + 1):
        if sg < NSG:
            emit_chain(sg)
        if sg >= 1:
            emit_compute(sg - 1)
            done = sg - 1
            if done % 2 == 1:
                kc = done // 2
                w = [tok for g in range(2 * kc, 2 * kc + 2) for tok in eost_grp[g]]
                op("gp", lambda e, kc=kc: e.collective_compute(
                    "AllGather", ALU.bypass, replica_groups=[list(range(NCORE))],
                    ins=[eomy[512 * kc:512 * (kc + 1), :]],
                    outs=[eoall[4096 * kc:4096 * (kc + 1), :]]),
                    waits=w, inc="cEO")
                cceo_n += 1
    t_cceo = ("cEO", cceo_n)

    gb = lambda i, r: LFB[:, 4096 * i + 1024 * r:4096 * i + 1024 * (r + 1)]
    x1r = lambda r: AXA[:, 4096 + 1024 * r:4096 + 1024 * (r + 1)]
    yb = lambda r: AXA[:, 8192 + 1024 * r:8192 + 1024 * (r + 1)]
    ob = lambda r: AXA[:, 12288 + 1024 * r:12288 + 1024 * (r + 1)]
    g_war = {}
    y_war = {}
    o_war = {}
    out_st = []
    for tau in range(NTILE):
        r = tau % 2
        r4 = tau % 4
        t_g0 = op("gp", lambda e, tau=tau, r4=r4: e.indirect_dma_start(
            out=gb(0, r4), out_offset=None, in_=eoall,
            in_offset=IndirectOffsetOnAxis(of1[:, tau:tau + 1], axis=0),
            bounds_check=REGS['b32k'], oob_is_err=False),
            waits=[t_cceo, of_toks[0], g_war.get((0, r4))], inc="dG", amt=16)
        t_g1 = op("gp", lambda e, tau=tau, r4=r4: e.indirect_dma_start(
            out=gb(1, r4), out_offset=None, in_=eoall,
            in_offset=IndirectOffsetOnAxis(of2[:, tau:tau + 1], axis=0),
            bounds_check=REGS['b32k'], oob_is_err=False),
            waits=[t_cceo, of_toks[1], g_war.get((1, r4))], inc="dG", amt=16)
        t_xl = dma("ac", x1r(r), x1my[128 * tau:128 * (tau + 1), :],
                   waits=[x1st[tau], t_cceo, o_war.get(r)], sem="dA")
        t_y0 = op("ve", lambda e, tau=tau, r=r, r4=r4: e.tensor_scalar_mul(
            yb(r), gb(0, r4), sc1[:, tau:tau + 1]),
            waits=[t_g0, sc_toks[0], y_war.get(r)], inc="tVE")
        t_y1 = op("ve", lambda e, tau=tau, r=r, r4=r4: e.scalar_tensor_tensor(
            yb(r), gb(1, r4), sc2[:, tau:tau + 1], yb(r), ALU.mult, ALU.add),
            waits=[t_y0, t_g1, sc_toks[1]], inc="tVE")
        g_war[(0, r4)] = t_y0
        g_war[(1, r4)] = t_y1
        t_o = op("ve", lambda e, r=r: e.tensor_add(ob(r), yb(r), x1r(r)),
                 waits=[t_y1, t_xl], inc="tVE")
        y_war[r] = t_o
        t_os = dma("sy", out_d[128 * tau:128 * (tau + 1), :], ob(r), waits=[t_o])
        o_war[r] = t_os
        out_st.append(t_os)

    if dbg:
        op("gp", lambda e: e.dma_start(out=d_x1, in_=x1my), waits=x1st, inc="dG", amt=16)
        op("gp", lambda e: e.dma_start(out=d_ptm, in_=probs_tm[:, :]), waits=[tk_probs_done], inc="dG", amt=16)
        op("gp", lambda e: e.dma_start(out=d_vscr, in_=vscr), waits=[vsr_ld[3][0], vsr_ld[3][1]], inc="dG", amt=16)
        op("gp", lambda e: e.dma_start(out=d_xsr, in_=xsr), waits=[tk_ccxs], inc="dG", amt=16)
        op("gp", lambda e: e.dma_start(out=d_ln2, in_=ln2my), waits=ln2st, inc="dG", amt=16)
        op("gp", lambda e: e.dma_start(out=d_lg, in_=ltT[:, :]), waits=lgall, inc="dG", amt=16)
        op("gp", lambda e: e.dma_start(out=d_pos[:, 0:16], in_=w1s[:, :]), waits=[t_po1], inc="dG", amt=16)
        op("gp", lambda e: e.dma_start(out=d_pos[:, 16:32], in_=w2s[:, :]), waits=[t_po2], inc="dG", amt=16)
        op("gp", lambda e: e.dma_start(out=d_pos[:, 32:48], in_=sc1[:, :]), waits=[sc_toks[0]], inc="dG", amt=16)
        op("gp", lambda e: e.dma_start(out=d_pos[:, 48:64], in_=sc2[:, :]), waits=[sc_toks[1]], inc="dG", amt=16)
        op("gp", lambda e: e.dma_start(out=d_tbl, in_=tblrs), waits=[t_rs_], inc="dG", amt=16)
        op("gp", lambda e: e.dma_start(out=d_sm[0:16, 0:8], in_=colsum[:, :]), waits=[t_cse], inc="dG", amt=16)
        op("gp", lambda e: e.dma_start(out=d_sm[0:4, 8:16], in_=cnt_sb[:, :]), waits=[t_cnte], inc="dG", amt=16)
        op("gp", lambda e: e.dma_start(out=d_sm[0:32, 16:24], in_=cag[:, :]), waits=[t_cgl], inc="dG", amt=16)
        op("gp", lambda e: e.dma_start(out=d_sm[0:4, 24:32], in_=base[:, :]), waits=[t_bse], inc="dG", amt=16)
        op("gp", lambda e: e.dma_start(out=d_sm[0:16, 32:40], in_=rall[:, :]), waits=[t_re], inc="dG", amt=16)
        op("gp", lambda e: e.dma_start(out=d_sm[0:1, 40:168], in_=rrow[0:1, :]), waits=[t_rld], inc="dG", amt=16)
        op("gp", lambda e: e.dma_start(out=d_sm[:, 168:296], in_=pfull[:, :]), waits=[t_pf], inc="dG", amt=16)
        op("gp", lambda e: e.dma_start(out=d_eo, in_=eomy), waits=eo_st, inc="dG", amt=16)

    final_dS = ("dS", CNT["dS"])
    final_dG = ("dG", CNT["dG"])
    op("sy", lambda e: e.nop(), waits=[final_dS])
    op("gp", lambda e: e.nop(), waits=[final_dG, ("cEO", 1)])

    with nc.Block() as blk:
        def runner(code):
            def body(eng):
                if code == "gp":
                    REGS['b16k'] = eng.alloc_register("b16k")
                    eng.reg_mov(REGS['b16k'], NCORE * NTOK - 1)
                    REGS['b32k'] = eng.alloc_register("b32k")
                    eng.reg_mov(REGS['b32k'], NE * CAP - 1)
                seen = {}
                for (ec, waits, emit, inc) in EV:
                    if ec != code:
                        continue
                    for (s, v) in waits:
                        if seen.get(s, 0) < v:
                            eng.wait_ge(SEM[s], v)
                            seen[s] = v
                    inst = emit(eng)
                    if inc is not None:
                        inst.then_inc(SEM[inc[0]], inc[1])
            return body
        blk.vector(runner("ve"))
        blk.tensor(runner("te"))
        blk.scalar(runner("ac"))
        blk.sync(runner("sy"))
        blk.gpsimd(runner("gp"))
    ES.close()
    return nc


def kernel(L0_attn_x, L0_attn_Wq, L0_attn_Wk, L0_attn_Wv, L0_ffn_rw1, L0_ffn_rb1,
           L0_ffn_rw2, L0_ffn_rb2, L0_ffn_ew1, L0_ffn_eb1, L0_ffn_ew2, L0_ffn_eb2,
           _dbg=False):
    x = np.ascontiguousarray(np.asarray(L0_attn_x, np.float32))
    rw1 = np.asarray(L0_ffn_rw1, np.float32)
    rw1_r = np.ascontiguousarray(rw1.reshape(8, 128, HID).transpose(1, 0, 2).reshape(128, 8 * HID))
    rw2_r = np.ascontiguousarray(np.asarray(L0_ffn_rw2, np.float32).reshape(32, 128, NE)
                                 .transpose(1, 0, 2).reshape(128, 32 * NE))
    rb1_r = np.ascontiguousarray(np.asarray(L0_ffn_rb1, np.float32).reshape(32, 128).T)
    rb2_c = np.ascontiguousarray(np.asarray(L0_ffn_rb2, np.float32).reshape(8, 1))
    ew1 = np.asarray(L0_ffn_ew1, np.float32)
    ew2 = np.asarray(L0_ffn_ew2, np.float32)
    eb1 = np.asarray(L0_ffn_eb1, np.float32)
    eb2 = np.asarray(L0_ffn_eb2, np.float32)
    wq = np.ascontiguousarray(np.asarray(L0_attn_Wq, np.float32))
    wk = np.ascontiguousarray(np.asarray(L0_attn_Wk, np.float32))
    wv = np.ascontiguousarray(np.asarray(L0_attn_Wv, np.float32))

    nc = build(dbg=_dbg)
    in_maps = []
    for j in range(NCORE):
        xj = np.ascontiguousarray(x[:, TPC * j:TPC * (j + 1), :].reshape(NTOK, E))
        ew1_r = np.ascontiguousarray(ew1[j].reshape(8, 128, HID).transpose(1, 0, 2).reshape(128, 8 * HID))
        ew2_r = np.ascontiguousarray(ew2[j].reshape(32, 128, E).transpose(1, 0, 2).reshape(128, 32 * E))
        eb1_r = np.ascontiguousarray(eb1[j].reshape(32, 128).T)
        eb2_r = np.ascontiguousarray(eb2[j].reshape(8, 128).T)
        in_maps.append(dict(
            x_my=xj, wq=wq, wk=wk, wv=wv,
            rw1r=rw1_r, rw2r=rw2_r, rb1r=rb1_r, rb2c=rb2_c,
            ew1r=ew1_r, ew2r=ew2_r, eb1r=eb1_r, eb2r=eb2_r,
            cpk=make_constpack(j),
        ))
    res = run_bass_kernel_spmd(nc, in_maps, core_ids=list(range(NCORE)))
    out = np.empty((C, T, E), np.float32)
    for j in range(NCORE):
        out[:, TPC * j:TPC * (j + 1), :] = res.results[j]["out"].reshape(C, TPC, E)
    if _dbg:
        return out, res
    return out
